# version 06
# baseline (speedup 1.0000x reference)
import numpy as np
import ml_dtypes
from contextlib import ExitStack

import os
import concourse.bass as bass
import concourse.mybir as mybir
from concourse.bass_utils import run_bass_kernel_spmd

F32 = mybir.dt.float32
BF16 = mybir.dt.bfloat16
AF = mybir.ActivationFunctionType
ALU = mybir.AluOpType

D, KD, VD, H, DK, DV, LR = 2048, 1024, 2048, 16, 64, 128, 16
TC, W, C = 4096, 512, 128
NW, NCH, NG = TC // W, W // C, TC // C
NKT, NQT, NVT = D // 128, KD // 128, VD // 128
E, HH = 8, 1408
NHT = HH // 128
EPS = 1e-5
LN8 = float(np.log(DK ** -0.5))
BF = ml_dtypes.bfloat16


class Bld:

    ENGS = ("sync", "gpsimd", "scalar", "vector", "tensor")

    def __init__(self, nc, ctx):
        self.nc = nc
        self.ctx = ctx
        self.q = {e: [] for e in self.ENGS}
        self.h = {}
        self.cnt = {}
        for s in self.ENGS:
            self.h[s] = ctx.enter_context(nc.semaphore("S_" + s))
            self.cnt[s] = 0
        self.lastwait = {e: {} for e in self.ENGS}

    def dsem(self, name):
        if name not in self.h:
            self.h[name] = self.ctx.enter_context(self.nc.semaphore("S_" + name))
            self.cnt[name] = 0
        return name

    def _waits(self, eng, after):
        best = {}
        for t in after:
            if t is None:
                continue
            s, v = t
            if v > best.get(s, 0):
                best[s] = v
        waits = []
        for s, v in best.items():
            if v > self.lastwait[eng].get(s, 0):
                waits.append((s, v))
                self.lastwait[eng][s] = v
        return waits

    def emit(self, eng, fn, after=(), tok=False, dma=None):
        waits = self._waits(eng, after)
        inc = None
        if dma:
            s = self.dsem(dma)
            self.cnt[s] += 16
            inc = (s, 16)
        elif tok:
            self.cnt[eng] += 1
            inc = (eng, 1)
        Hh = self.h

        def run(engine, waits=waits, inc=inc, fn=fn):
            for s, v in waits:
                engine.wait_ge(Hh[s], v)
            ins = fn(engine)
            if inc is not None:
                ins.then_inc(Hh[inc[0]], inc[1])

        self.q[eng].append(run)
        return (inc[0], self.cnt[inc[0]]) if inc else None

    def waitonly(self, eng, after):
        waits = self._waits(eng, after)
        if not waits:
            return
        Hh = self.h

        def run(engine, waits=waits):
            for s, v in waits:
                engine.wait_ge(Hh[s], v)

        self.q[eng].append(run)

    def alltoks(self):
        return [(s, self.cnt[s]) for s in self.cnt if self.cnt[s] > 0]

    def barrier(self, engs=None):
        toks = self.alltoks()
        for e in (engs or self.ENGS):
            self.waitonly(e, toks)

    def replay(self, blk):
        for name in self.ENGS:
            steps = self.q[name]
            getattr(blk, name)(lambda eng, steps=steps: [f(eng) for f in steps])


def rt(ap, kt):
    return ap.rearrange("p (k t) -> p k t", k=kt)


def phase1(b, io, per, windows=NW, stages=("ln", "gate", "gate2", "gate3", "gate4", "proj")):
    nc = b.nc
    with ExitStack() as ph:
        pe_ = ph.enter_context
        wq_sb = pe_(nc.sbuf_tensor("p1_wq", [128, NKT * 1024], BF16))
        wk_sb = pe_(nc.sbuf_tensor("p1_wk", [128, NKT * 1024], BF16))
        wgk1_sb = pe_(nc.sbuf_tensor("p1_wgk1", [128, NKT * 16], BF16))
        wgk2_sb = pe_(nc.sbuf_tensor("p1_wgk2", [16, KD], BF16))
        bgk2_sb = pe_(nc.sbuf_tensor("p1_bgk2", [1, KD], BF16))
        xT = pe_(nc.sbuf_tensor("p1_xT", [128, NKT * 512], F32))
        tmpS = pe_(nc.sbuf_tensor("p1_tmpS", [128, 2 * 512], BF16))
        xnT = pe_(nc.sbuf_tensor("p1_xnT", [128, NKT * 512], BF16))
        rows = pe_(nc.sbuf_tensor("p1_rows", [1, 4 * 512], F32))
        gk1 = pe_(nc.sbuf_tensor("p1_gk1", [16, 512], BF16))
        uB = pe_(nc.sbuf_tensor("p1_u", [128, 512], BF16))
        euB = pe_(nc.sbuf_tensor("p1_eu", [128, 512], BF16))
        l1B = pe_(nc.sbuf_tensor("p1_l1", [128, 512], BF16))
        rB = pe_(nc.sbuf_tensor("p1_r", [128, 512], BF16))
        T1T = pe_(nc.sbuf_tensor("p1_T1T", [128, NQT * 512], BF16))
        A_sb = pe_(nc.sbuf_tensor("p1_A", [128, NQT * 512], F32))
        sq = A_sb[:, :].bitcast(BF16)
        e1 = pe_(nc.sbuf_tensor("p1_e1", [128, NQT * 512], BF16))
        e2 = pe_(nc.sbuf_tensor("p1_e2", [128, NQT * 512], BF16))
        sqc = pe_(nc.sbuf_tensor("p1_sqc", [128, NCH * 8], F32))
        skc = pe_(nc.sbuf_tensor("p1_skc", [128, NCH * 8], F32))
        dcol = pe_(nc.sbuf_tensor("p1_dcol", [128, 8], F32))
        qt = pe_(nc.sbuf_tensor("p1_qt", [128, NQT * 512], BF16))
        kt_sb = pe_(nc.sbuf_tensor("p1_kt", [128, NQT * 512], BF16))
        P = [pe_(nc.psum_tensor(f"p1_P{i}", [128, 512], F32)) for i in range(6)]

        zeros = per["zeros"]
        ones_col = per["ones_col"]
        ones1f = per["ones1f"]
        onesrow_bf = per["onesrow_bf"]

        tw = []
        tw.append(b.emit("gpsimd", lambda g: g.dma_start(
            out=wq_sb[:, :].rearrange("p (k n) -> p k n", k=NKT),
            in_=io["wq"].rearrange("(k p) n -> p k n", p=128)), dma="g_p1w"))
        tw.append(b.emit("gpsimd", lambda g: g.dma_start(
            out=wk_sb[:, :].rearrange("p (k n) -> p k n", k=NKT),
            in_=io["wk"].rearrange("(k p) n -> p k n", p=128)), dma="g_p1w"))
        tw.append(b.emit("gpsimd", lambda g: g.dma_start(
            out=wgk1_sb[:, :].rearrange("p (k n) -> p k n", k=NKT),
            in_=io["wgk1"].rearrange("(k p) n -> p k n", p=128)), dma="g_p1w"))
        tw.append(b.emit("gpsimd", lambda g: g.dma_start(
            out=wgk2_sb[:, :], in_=io["wgk2"]), dma="g_p1w"))
        tw.append(b.emit("gpsimd", lambda g: g.dma_start(
            out=bgk2_sb[:, :], in_=io["bgk2"]), dma="g_p1w"))
        wtok = tw[-1]

        tz0 = b.emit("vector", lambda v: v.memset(sqc[:, 0:8], float(np.exp(LN8))), tok=True)
        tz1 = b.emit("vector", lambda v: v.memset(skc[:, 0:8], 1.0), tok=True)
        tlam = b.emit("vector", lambda v: v.memset(per["lamrun"][:, :], 1.0), tok=True)

        tq_done = tk_done = txn = txnout = None
        for w in range(windows):
            tx = b.emit("sync", lambda s, w=w: s.dma_start(
                out=rt(xT[:, :], NKT),
                in_=io["xT"][:, w * 512:(w + 1) * 512].rearrange("(k p) t -> p k t", p=128)),
                after=[txn], dma="h_p1x")
            tsq = b.emit("scalar", lambda s: s.activation(sq, xT[:, :], AF.Square),
                         after=[tx, tk_done], tok=True)
            mmt = mmt2 = None
            for k in range(NKT):
                mmt = b.emit("tensor", lambda t, k=k: t.matmul(
                    P[0][0:1, :], per["ones_colf"][:, :], xT[:, k * 512:(k + 1) * 512],
                    start=(k == 0), stop=(k == NKT - 1)),
                    after=[tx, tq_done, tk_done] if k == 0 else (), tok=(k == NKT - 1))
            for k in range(NKT):
                mmt2 = b.emit("tensor", lambda t, k=k: t.matmul(
                    P[1][0:1, :], ones_col[:, :],
                    sq.rearrange("p (k t) -> p k t", k=NKT)[:, k, :],
                    start=(k == 0), stop=(k == NKT - 1)),
                    after=[tsq] if k == 0 else (), tok=(k == NKT - 1))
            t1 = b.emit("vector", lambda v: v.tensor_scalar_mul(
                rows[0:1, 0:512], P[0][0:1, :], 1.0 / D), after=[mmt], tok=True)
            t2 = b.emit("vector", lambda v: v.tensor_scalar_mul(
                rows[0:1, 512:1024], P[1][0:1, :], 1.0 / D), after=[mmt2], tok=True)
            t3 = b.emit("vector", lambda v: v.tensor_tensor(
                rows[0:1, 1024:1536], rows[0:1, 0:512], rows[0:1, 0:512], ALU.mult),
                after=[t1], tok=True)
            t4 = b.emit("vector", lambda v: v.tensor_tensor(
                rows[0:1, 1536:2048], rows[0:1, 512:1024], rows[0:1, 1024:1536], ALU.subtract),
                after=[t2, t3], tok=True)
            t5 = b.emit("scalar", lambda s: s.activation(
                rows[0:1, 512:1024], rows[0:1, 1536:2048], AF.Ln, bias=per["cbias"][0:1, 0:1]),
                after=[t4], tok=True)
            t6 = b.emit("scalar", lambda s: s.activation(
                rows[0:1, 1024:1536], rows[0:1, 512:1024], AF.Exp, scale=-0.5),
                after=[t5, t3], tok=True)
            t7 = b.emit("vector", lambda v: v.tensor_tensor(
                rows[0:1, 1536:2048], rows[0:1, 0:512], rows[0:1, 1024:1536], ALU.mult),
                after=[t6, t4], tok=True)
            tb1 = b.emit("tensor", lambda t: t.matmul(
                P[2][:, :], ones1f[:, :], rows[0:1, 1024:1536], start=True, stop=True),
                after=[t6, txn], tok=True)
            tb2 = b.emit("tensor", lambda t: t.matmul(
                P[3][:, :], ones1f[:, :], rows[0:1, 1536:2048], start=True, stop=True),
                after=[t7], tok=True)
            txn = None
            tprev = {0: None, 1: None}
            for k in range(NKT):
                par = k % 2
                ta = b.emit("vector", lambda v, k=k, par=par: v.tensor_tensor(
                    tmpS[:, par * 512:(par + 1) * 512], xT[:, k * 512:(k + 1) * 512],
                    P[2][:, :], ALU.mult),
                    after=[tb1, mmt, mmt2, tprev[par], txnout], tok=True)
                txn = b.emit("vector", lambda v, k=k, par=par: v.tensor_tensor(
                    xnT[:, k * 512:(k + 1) * 512], tmpS[:, par * 512:(par + 1) * 512],
                    P[3][:, :], ALU.subtract), after=[tb2, ta], tok=True)
                tprev[par] = txn
            txnout = b.emit("sync", lambda s, w=w: s.dma_start(
                out=io["sc_xnT"][w], in_=xnT[:, :]), after=[txn], dma="h_p1xn")

            if "gate" not in stages:
                continue
            tg = None
            for k in range(NKT):
                tg = b.emit("tensor", lambda t, k=k: t.matmul(
                    P[4][0:16, :], wgk1_sb[:, k * 16:(k + 1) * 16],
                    xnT[:, k * 512:(k + 1) * 512],
                    start=(k == 0), stop=(k == NKT - 1)),
                    after=[txn, wtok] if k == 0 else (), tok=(k == NKT - 1))
            tg1 = b.emit("scalar", lambda s: s.copy(gk1[:, :], P[4][0:16, :]),
                         after=[tg], tok=True)
            if "gate2" not in stages:
                continue
            tT1 = {}
            bank_free = {0: tg1, 1: None}
            pu = peu = pl1 = pT1 = pr = None
            for mb in range(NQT):
                pp = P[4 + (mb % 2)]
                tm0 = b.emit("tensor", lambda t, mb=mb, pp=pp: t.matmul(
                    pp[:, :], wgk2_sb[0:16, mb * 128:(mb + 1) * 128], gk1[:, :],
                    start=True, stop=False),
                    after=[tg1, bank_free[mb % 2]])
                tm = b.emit("tensor", lambda t, mb=mb, pp=pp: t.matmul(
                    pp[:, :], bgk2_sb[0:1, mb * 128:(mb + 1) * 128], onesrow_bf[:, :],
                    start=False, stop=True), tok=True)
                tu = b.emit("scalar", lambda s, pp=pp: s.activation(
                    uB[:, :], pp[:, :], AF.Abs), after=[tm, peu, pr], tok=True)
                teu = b.emit("scalar", lambda s: s.activation(
                    euB[:, :], uB[:, :], AF.Exp, scale=-1.0), after=[tu, pl1], tok=True)
                tl1 = b.emit("scalar", lambda s: s.activation(
                    l1B[:, :], euB[:, :], AF.Ln, bias=per["cbias"][:, 1:2]),
                    after=[teu, pT1], tok=True)
                tr = b.emit("vector", lambda v, pp=pp: v.scalar_tensor_tensor(
                    rB[:, :], uB[:, :], 1.0, pp[:, :], ALU.mult, ALU.subtract),
                    after=[tm, tu, pT1], tok=True)
                tT1[mb] = b.emit("vector", lambda v, mb=mb: v.scalar_tensor_tensor(
                    T1T[:, mb * 512:(mb + 1) * 512], rB[:, :], 0.5, l1B[:, :],
                    ALU.mult, ALU.add), after=[tl1, tr], tok=True)
                bank_free[mb % 2] = tT1[mb]
                pu, peu, pl1, pT1, pr = tu, teu, tl1, tT1[mb], tr
            if "gate3" not in stages:
                continue
            tA = []
            for k in range(NQT):
                tA.append(b.emit("vector", lambda v, k=k: v.tensor_tensor_scan(
                    A_sb[:, k * 512:(k + 1) * 512], T1T[:, k * 512:(k + 1) * 512],
                    zeros[:, :], 0.0, ALU.add, ALU.add), after=[tT1[k]], tok=True))
            if "gate4" not in stages:
                continue
            te1 = b.emit("scalar", lambda s: s.activation(
                e1[:, :], A_sb[:, :], AF.Exp, scale=-1.0 / 16), after=tA, tok=True)
            te2 = b.emit("scalar", lambda s: s.activation(
                e2[:, :], A_sb[:, :], AF.Exp, scale=1.0 / 16), after=tA, tok=True)
            tsc = {0: (tz0, tz1)}
            for c in range(1, NCH):
                a1 = b.emit("scalar", lambda s, c=c: s.activation(
                    sqc[:, c * 8:(c + 1) * 8], rt(A_sb, NQT)[:, :, c * C - 1],
                    AF.Exp, scale=1.0 / 16, bias=per["cbias"][:, 2:3]), after=tA, tok=True)
                a2 = b.emit("scalar", lambda s, c=c: s.activation(
                    skc[:, c * 8:(c + 1) * 8], rt(A_sb, NQT)[:, :, c * C - 1],
                    AF.Exp, scale=-1.0 / 16), after=tA, tok=True)
                tsc[c] = (a1, a2)
            tdprev = None
            for c in range(NCH):
                g = w * NCH + c
                if c == 0:
                    td = b.emit("vector", lambda v: v.tensor_copy(
                        dcol[:, :], rt(A_sb, NQT)[:, :, C - 1]),
                        after=tA + [tdprev], tok=True)
                else:
                    td = b.emit("vector", lambda v, c=c: v.tensor_tensor(
                        dcol[:, :], rt(A_sb, NQT)[:, :, (c + 1) * C - 1],
                        rt(A_sb, NQT)[:, :, c * C - 1], ALU.subtract),
                        after=tA + [tdprev], tok=True)
                texp = b.emit("scalar", lambda s, g=g: s.activation(
                    per["expaC"][:, g * 8:(g + 1) * 8], dcol[:, :], AF.Exp, scale=-1.0 / 16),
                    after=[td], tok=True)
                tdprev = texp
                tlc = b.emit("vector", lambda v, g=g: v.tensor_copy(
                    per["lam"][:, g * 8:(g + 1) * 8], per["lamrun"][:, :]),
                    after=[tlam], tok=True)
                tlam = b.emit("vector", lambda v, g=g: v.tensor_tensor(
                    per["lamrun"][:, :], per["lamrun"][:, :],
                    per["expaC"][:, g * 8:(g + 1) * 8], ALU.mult),
                    after=[texp, tlc], tok=True)

            if "proj" not in stages:
                continue
            def proj(wsb, esb, scsb, outsb, pA, pB, edep, war):
                tev = None
                for pair in range(NQT // 2):
                    pmm = {}
                    for half, pp in ((0, pA), (1, pB)):
                        mb = pair * 2 + half
                        tmm = None
                        for k in range(NKT):
                            tmm = b.emit("tensor", lambda t, mb=mb, k=k, pp=pp: t.matmul(
                                pp[:, :], wsb[:, k * 1024 + mb * 128: k * 1024 + (mb + 1) * 128],
                                xnT[:, k * 512:(k + 1) * 512],
                                start=(k == 0), stop=(k == NKT - 1)),
                                after=[txn, wtok, tev, war] if k == 0 else (),
                                tok=(k == NKT - 1))
                        pmm[half] = tmm
                    for half, pp in ((0, pA), (1, pB)):
                        mb = pair * 2 + half
                        for c in range(NCH):
                            tev = b.emit("vector", lambda v, mb=mb, c=c, pp=pp: v.scalar_tensor_tensor(
                                outsb[:, mb * 512 + c * C: mb * 512 + (c + 1) * C],
                                esb[:, mb * 512 + c * C: mb * 512 + (c + 1) * C],
                                scsb[:, c * 8 + mb: c * 8 + mb + 1],
                                pp[:, c * C:(c + 1) * C], ALU.mult, ALU.mult),
                                after=[pmm[half], edep, tsc[c][0], tsc[c][1]], tok=True)
                return tev

            tq_done = proj(wq_sb, e1, sqc, qt, P[0], P[1], te1, txn)
            tk_done = proj(wk_sb, e2, skc, kt_sb, P[2], P[3], te2, txn)
            tdq = b.emit("sync", lambda s, w=w: s.dma_start(
                out=io["sc_qtT"][w], in_=qt[:, :]), after=[tq_done], dma="h_p1q")
            tdk = b.emit("sync", lambda s, w=w: s.dma_start(
                out=io["sc_ktT"][w], in_=kt_sb[:, :]), after=[tk_done], dma="h_p1k")
            tq_done = tdq
            tk_done = tdk
        if os.environ.get("T_P1DBG"):
            nc2 = b.nc
            dT1 = nc2.dram_tensor("dbg_T1T", [128, NQT * 512], BF16, kind="ExternalOutput").ap()
            dA = nc2.dram_tensor("dbg_A", [128, NQT * 512], F32, kind="ExternalOutput").ap()
            de1 = nc2.dram_tensor("dbg_e1", [128, NQT * 512], BF16, kind="ExternalOutput").ap()
            de2 = nc2.dram_tensor("dbg_e2", [128, NQT * 512], BF16, kind="ExternalOutput").ap()
            dsq = nc2.dram_tensor("dbg_sqc", [128, NCH * 8], F32, kind="ExternalOutput").ap()
            dsk = nc2.dram_tensor("dbg_skc", [128, NCH * 8], F32, kind="ExternalOutput").ap()
            toks = b.alltoks()
            b.waitonly("sync", toks)
            b.emit("sync", lambda s: s.dma_start(out=dT1, in_=T1T[:, :]), dma="h_dbg2")
            b.emit("sync", lambda s: s.dma_start(out=dA, in_=A_sb[:, :]), dma="h_dbg2")
            b.emit("sync", lambda s: s.dma_start(out=de1, in_=e1[:, :]), dma="h_dbg2")
            b.emit("sync", lambda s: s.dma_start(out=de2, in_=e2[:, :]), dma="h_dbg2")
            b.emit("sync", lambda s: s.dma_start(out=dsq, in_=sqc[:, :]), dma="h_dbg2")
            b.emit("sync", lambda s: s.dma_start(out=dsk, in_=skc[:, :]), dma="h_dbg2")
        b.barrier()


POS = [(h % 2) * 8 + h // 2 for h in range(H)]


def phase2(b, io, per, windows=NW, stage_w=True):
    nc = b.nc
    with ExitStack() as ph:
        pe_ = ph.enter_context
        wv_sb = pe_(nc.sbuf_tensor("p2_wv", [128, NKT * VD], BF16))
        xnT2 = pe_(nc.sbuf_tensor("p2_xnT", [128, 2 * NKT * 512], BF16))
        qt2 = pe_(nc.sbuf_tensor("p2_qt", [128, 2 * NQT * 512], BF16))
        kt2 = pe_(nc.sbuf_tensor("p2_kt", [128, 2 * NQT * 512], BF16))
        ktn = pe_(nc.sbuf_tensor("p2_ktn", [128, NQT * 128], BF16))
        V_sb = pe_(nc.sbuf_tensor("p2_V", [128, VD], BF16))
        PTm = pe_(nc.sbuf_tensor("p2_PTm", [128, 2 * 128], BF16))
        S_sb = pe_(nc.sbuf_tensor("p2_S", [128, NQT * 128], F32))
        S_bf = pe_(nc.sbuf_tensor("p2_Sbf", [128, NQT * 128], BF16))
        Stmp = pe_(nc.sbuf_tensor("p2_Stmp", [128, NQT * 128], F32))
        OTw = pe_(nc.sbuf_tensor("p2_OTw", [128, 16 * 512], BF16))
        wstg = pe_(nc.sbuf_tensor("p2_wstg", [128, 6 * D], BF16))
        pVO = pe_(nc.psum_tensor("p2_pVO", [128, 2048], F32))
        pPT = [pe_(nc.psum_tensor(f"p2_pPT{i}", [128, 512], F32)) for i in range(2)]
        pKU = pe_(nc.psum_tensor("p2_pKU", [128, 1024], F32))
        pK = pKU[:, 0:512].bitcast(BF16)

        tw = b.emit("gpsimd", lambda g: g.dma_start(
            out=wv_sb[:, :].rearrange("p (k n) -> p k n", k=NKT),
            in_=io["wv"].rearrange("(k p) n -> p k n", p=128)), dma="g_p2w")
        b.emit("vector", lambda v: v.memset(S_sb[:, :], 0.0), tok=True)
        tS0 = b.emit("vector", lambda v: v.memset(S_bf[:, :], 0.0), tok=True)

        stage_jobs = []
        if stage_w:
            for e_ in range(E):
                for which in (0, 1):
                    src = (io["w1T"][e_].rearrange("(k p) n -> p k n", p=128) if which == 0
                           else io["w2T"][e_].rearrange("(k p) n -> p k n", p=128))
                    dst = io["sc_w1b"][e_] if which == 0 else io["sc_w2b"][e_]
                    kk = NKT if which == 0 else NHT
                    nn = HH if which == 0 else D
                    h1 = kk // 2
                    stage_jobs.append((src[:, 0:h1, :], dst[:, 0:h1 * nn], h1))
                    stage_jobs.append((src[:, h1:kk, :], dst[:, h1 * nn:kk * nn], kk - h1))
        tstg = None

        def stage_some(njobs, gate_tok):
            nonlocal tstg
            for _ in range(njobs):
                if not stage_jobs:
                    return
                src, dst, kk = stage_jobs.pop(0)
                nel = src.shape[1] * src.shape[2]
                tg_ = b.emit("gpsimd", lambda g, src=src, kk=kk, nel=nel: g.dma_start(
                    out=wstg[:, 0:nel].rearrange("p (k n) -> p k n", k=kk),
                    in_=src), after=[tstg, gate_tok], dma="g_wstg")
                tstg = b.emit("scalar", lambda s, dst=dst, nel=nel: s.dma_start(
                    out=dst, in_=wstg[:, 0:nel]), after=[tg_], dma="h_wstg")

        tVml = {0: None, 1: None}
        tOel = None
        tSul = tS0
        txn = tqt = tkt = {}
        tOdma = None
        tVml_w = {0: None, 1: None}
        tqk_w = {0: None, 1: None}
        txn_t = {}
        tqt_t = {}
        tkt_t = {}
        for w in range(windows):
            wp = w % 2
            xnT = xnT2[:, wp * NKT * 512:(wp + 1) * NKT * 512]
            qt = qt2[:, wp * NQT * 512:(wp + 1) * NQT * 512]
            kt_sb = kt2[:, wp * NQT * 512:(wp + 1) * NQT * 512]
            txn = b.emit("sync", lambda s, w=w, xnT=xnT: s.dma_start(
                out=xnT, in_=io["sc_xnT"][w]), after=[tVml_w[wp]], dma="h_p2xn")
            tqt = b.emit("sync", lambda s, w=w, qt=qt: s.dma_start(
                out=qt, in_=io["sc_qtT"][w]), after=[tqk_w[wp]], dma="h_p2q")
            tkt = b.emit("sync", lambda s, w=w, kt_sb=kt_sb: s.dma_start(
                out=kt_sb, in_=io["sc_ktT"][w]), after=[tqk_w[wp]], dma="h_p2k")
            for c in range(NCH):
                g = w * NCH + c
                tV = None
                for nb in range(4):
                    for k in range(NKT):
                        tV = b.emit("tensor", lambda t, k=k, nb=nb, c=c, xnT=xnT: t.matmul(
                            pVO[:, nb * 512:(nb + 1) * 512],
                            xnT[:, k * 512 + c * 128: k * 512 + (c + 1) * 128],
                            wv_sb[:, k * VD + nb * 512: k * VD + (nb + 1) * 512],
                            start=(k == 0), stop=(k == NKT - 1)),
                            after=[txn, tw, tOel] if (k == 0 and nb == 0) else (),
                            tok=(k == NKT - 1))
                tVe = b.emit("vector", lambda v: v.tensor_copy(V_sb[:, :], pVO[:, :]),
                             after=[tV], tok=True)
                tkn = None
                for mb in range(NQT):
                    tkn = b.emit("tensor", lambda t, mb=mb, c=c, kt_sb=kt_sb: t.transpose(
                        pK[:, mb * 128:(mb + 1) * 128],
                        kt_sb[:, mb * 512 + c * 128: mb * 512 + (c + 1) * 128],
                        per["ident"][:, :]),
                        after=[tkt, tSul] if mb == 0 else (), tok=(mb == NQT - 1))
                tkne = b.emit("vector", lambda v: v.tensor_copy(ktn[:, :], pK[:, :]),
                              after=[tkn], tok=True)
                tPTm = {0: None, 1: None}
                tOx = {}
                tOi = {}
                for h in range(H):
                    mb, off, par, pos = h // 2, 64 * (h % 2), h % 2, POS[h]
                    tPT = b.emit("tensor", lambda t, mb=mb, off=off, par=par, c=c, kt_sb=kt_sb, qt=qt: t.matmul(
                        pPT[par][:, 0:128],
                        kt_sb[off:off + 64, mb * 512 + c * 128: mb * 512 + (c + 1) * 128],
                        qt[off:off + 64, mb * 512 + c * 128: mb * 512 + (c + 1) * 128],
                        start=True, stop=True),
                        after=[tqt, tkt, tPTm[par], tOx.get(h - 2)], tok=True)
                    tPTm[par] = b.emit("vector", lambda v, par=par: v.tensor_tensor(
                        PTm[:, par * 128:(par + 1) * 128], pPT[par][:, 0:128],
                        per["cmask"][:, :], ALU.mult), after=[tPT], tok=True)
                    tOi[h] = b.emit("tensor", lambda t, mb=mb, off=off, pos=pos, c=c, qt=qt: t.matmul(
                        pVO[:, pos * 128:(pos + 1) * 128],
                        S_bf[off:off + 64, mb * 128:(mb + 1) * 128],
                        qt[off:off + 64, mb * 512 + c * 128: mb * 512 + (c + 1) * 128],
                        start=True, stop=False),
                        after=[tSul, tVe, tqt], tok=False)
                    tOx[h] = b.emit("tensor", lambda t, h=h, par=par, pos=pos: t.matmul(
                        pVO[:, pos * 128:(pos + 1) * 128],
                        V_sb[:, h * 128:(h + 1) * 128],
                        PTm[:, par * 128:(par + 1) * 128],
                        start=False, stop=True),
                        after=[tPTm[par]], tok=True)
                tU = None
                for h in range(H):
                    mb, off = h // 2, 64 * (h % 2)
                    tU = b.emit("tensor", lambda t, h=h, mb=mb, off=off: t.matmul(
                        pKU[off:off + 64, mb * 128:(mb + 1) * 128],
                        ktn[:, mb * 128 + off: mb * 128 + off + 64],
                        V_sb[:, h * 128:(h + 1) * 128],
                        start=True, stop=True,
                        tile_position=(0, off)),
                        after=[tkne, tVe] if h == 0 else (), tok=(h == H - 1))
                tSu1 = b.emit("vector", lambda v: v.tensor_tensor(
                    Stmp[:, :], S_sb[:, :], pKU[:, :], ALU.add),
                    after=[tU] + list(tOi.values()), tok=True)
                tSu = None
                for kti in range(NQT):
                    tSu = b.emit("vector", lambda v, kti=kti, g=g: v.tensor_scalar_mul(
                        S_sb[:, kti * 128:(kti + 1) * 128],
                        Stmp[:, kti * 128:(kti + 1) * 128],
                        per["expaC"][:, g * 8 + kti: g * 8 + kti + 1]),
                        after=[tSu1], tok=(kti == NQT - 1))
                tSul = b.emit("vector", lambda v: v.tensor_copy(S_bf[:, :], S_sb[:, :]),
                              after=[tSu] + list(tOi.values()), tok=True)
                tOel = b.emit("vector", lambda v, c=c: v.tensor_copy(
                    OTw[:, :].rearrange("p (q t) -> p q t", q=16)[:, :, c * 128:(c + 1) * 128],
                    pVO[:, :].rearrange("p (q t) -> p q t", q=16)),
                    after=[tOdma] + list(tOx.values()) + list(tOi.values()), tok=True)
                tVml_cur = tV
            tOdma = b.emit("sync", lambda s, w=w: s.dma_start(
                out=io["sc_OT"][w], in_=OTw[:, :]), after=[tOel], dma="h_p2o")
            tVml_w[wp] = tVml_cur
            tqk_w[wp] = tOel
            stage_some(4, tOdma)
        stage_some(99, None)
        b.emit("sync", lambda s: s.dma_start(
            out=io["cc_in"].ap(), in_=S_sb[:, :]), after=[tSul], dma="h_p2s")
        b.barrier()


def phase_cc(b, io, per):
    nc = b.nc
    tcc = b.emit("gpsimd", lambda g: g.collective_compute(
        "AllGather", ALU.bypass,
        replica_groups=[[0, 1], [2, 3], [4, 5], [6, 7]],
        ins=[io["cc_in"].ap()], outs=[io["cc_out"].ap()]), tok=True)
    t1 = b.emit("sync", lambda s: s.dma_start(
        out=per["Sin"][:, :], in_=io["cc_out"].ap()[0:128, :]), after=[tcc], dma="h_cc")
    t2 = b.emit("sync", lambda s: s.dma_start(
        out=per["smask"][:, :], in_=io["smask"]), dma="h_cc")
    t3 = b.emit("vector", lambda v: v.tensor_scalar_mul(
        per["Sin"][:, :], per["Sin"][:, :], per["smask"][:, 0:1]),
        after=[t1, t2], tok=True)
    b.emit("vector", lambda v: v.tensor_copy(per["Sinb"][:, :], per["Sin"][:, :]),
           after=[t3], tok=True)
    b.barrier()


def phase3a(b, io, per, windows=NW):
    nc = b.nc
    with ExitStack() as ph:
        pe_ = ph.enter_context
        wg_sb = pe_(nc.sbuf_tensor("p3_wg", [128, NKT * VD], BF16))
        xnT = pe_(nc.sbuf_tensor("p3_xnT", [128, NKT * 512], BF16))
        OTr = pe_(nc.sbuf_tensor("p3_OTr", [128, 16 * 512], BF16))
        qt = pe_(nc.sbuf_tensor("p3_qt", [128, NQT * 512], BF16))
        qtl = pe_(nc.sbuf_tensor("p3_qtl", [128, NQT * 128], BF16))
        silG = pe_(nc.sbuf_tensor("p3_silG", [128, 16 * 512], BF16))
        OTf = pe_(nc.sbuf_tensor("p3_OTf", [128, 16 * 512], BF16))
        OT2c = pe_(nc.sbuf_tensor("p3_OT2c", [128, 16 * 128], BF16))
        o1c = pe_(nc.sbuf_tensor("p3_o1c", [128, 16 * 128], BF16))
        og = pe_(nc.sbuf_tensor("p3_og", [128, 16 * 512], BF16))
        rln = pe_(nc.sbuf_tensor("p3_rln", [128, 16], F32))
        rstd = pe_(nc.sbuf_tensor("p3_rstd", [128, 16], F32))
        rstdT = pe_(nc.sbuf_tensor("p3_rstdT", [16, 128], F32))
        pG = [pe_(nc.psum_tensor(f"p3_pG{i}", [128, 512], F32)) for i in range(2)]
        pCB = pe_(nc.psum_tensor("p3_pCB", [128, 2048], F32))

        tw = b.emit("gpsimd", lambda g: g.dma_start(
            out=wg_sb[:, :].rearrange("p (k n) -> p k n", k=NKT),
            in_=io["wg"].rearrange("(k p) n -> p k n", p=128)), dma="g_p3w")

        tGl = tcorl = togl = togdma = None
        tpg0 = tpg1 = tsilfree = None
        for w in range(windows):
            txn = b.emit("sync", lambda s, w=w: s.dma_start(
                out=xnT[:, :], in_=io["sc_xnT"][w]), after=[tGl], dma="h_p3xn")
            tot = b.emit("sync", lambda s, w=w: s.dma_start(
                out=OTr[:, :], in_=io["sc_OT"][w]), after=[tcorl], dma="h_p3o")
            tqt = b.emit("sync", lambda s, w=w: s.dma_start(
                out=qt[:, :], in_=io["sc_qtT"][w]), after=[tcorl], dma="h_p3q")
            tse = {}
            for h in range(H):
                pp = pG[h % 2]
                tG = None
                for k in range(NKT):
                    tG = b.emit("tensor", lambda t, h=h, k=k, pp=pp: t.matmul(
                        pp[:, :], wg_sb[:, k * VD + h * 128: k * VD + (h + 1) * 128],
                        xnT[:, k * 512:(k + 1) * 512],
                        start=(k == 0), stop=(k == NKT - 1)),
                        after=[txn, tw, tse.get(h - 2), tpg0 if h == 0 else None,
                               tpg1 if h == 1 else None] if k == 0 else (),
                        tok=(k == NKT - 1))
                tse[h] = b.emit("scalar", lambda s, h=h, pp=pp: s.activation(
                    silG[:, POS[h] * 512:(POS[h] + 1) * 512], pp[:, :], AF.Silu),
                    after=[tG, tsilfree], tok=True)
            tGl = tG
            tsil = tse[H - 1]
            tstats = None
            tOTf = None
            for c in range(NCH):
                g = w * NCH + c
                tql = None
                for mb in range(NQT):
                    tql = b.emit("vector", lambda v, mb=mb, c=c, g=g: v.tensor_scalar_mul(
                        qtl[:, mb * 128:(mb + 1) * 128],
                        qt[:, mb * 512 + c * 128: mb * 512 + (c + 1) * 128],
                        per["lam"][:, g * 8 + mb: g * 8 + mb + 1]),
                        after=[tqt, tcorl], tok=(mb == NQT - 1))
                tcor = {}
                for h in range(H):
                    mb, off, pos = h // 2, 64 * (h % 2), POS[h]
                    tcor[h] = b.emit("tensor", lambda t, mb=mb, off=off, pos=pos: t.matmul(
                        pCB[:, pos * 128:(pos + 1) * 128],
                        per["Sinb"][off:off + 64, mb * 128:(mb + 1) * 128],
                        qtl[off:off + 64, mb * 128:(mb + 1) * 128],
                        start=True, stop=True),
                        after=[tql, togl] if h == 0 else (), tok=(h == H - 1))
                tOTf = b.emit("vector", lambda v, c=c: v.tensor_tensor(
                    rt(OTf, 16)[:, :, c * 128:(c + 1) * 128],
                    rt(OTr, 16)[:, :, c * 128:(c + 1) * 128],
                    pCB[:, :].rearrange("p (q t) -> p q t", q=16), ALU.add),
                    after=[tcor[H - 1], tot, togl], tok=True)
                tcorl = tOTf
                t2c = b.emit("scalar", lambda s, c=c: s.activation(
                    OT2c[:, :].rearrange("p (q t) -> p q t", q=16),
                    rt(OTf, 16)[:, :, c * 128:(c + 1) * 128], AF.Square),
                    after=[tOTf, tstats], tok=True)
                tst = {}
                for h in range(H):
                    pos = POS[h]
                    tst[h] = b.emit("tensor", lambda t, pos=pos, h=h: t.matmul(
                        pG[0][0:128, h:h + 1], OT2c[:, pos * 128:(pos + 1) * 128],
                        per["ones_col"][:, :], start=True, stop=True),
                        after=[t2c, tse.get(H - 1), tstats] if h == 0 else (),
                        tok=(h == H - 1))
                trl = b.emit("scalar", lambda s: s.activation(
                    rln[:, :], pG[0][0:128, 0:16], AF.Ln, scale=1.0 / DV,
                    bias=per["cbias"][:, 0:1]), after=[tst[H - 1]], tok=True)
                trs = b.emit("scalar", lambda s: s.activation(
                    rstd[:, :], rln[:, :], AF.Exp, scale=-0.5), after=[trl], tok=True)
                ttr = b.emit("tensor", lambda t: t.transpose(
                    pG[1][0:16, 0:128], rstd[:, :], per["identf"][:, :]),
                    after=[trs], tok=True)
                tte = b.emit("vector", lambda v: v.tensor_copy(
                    rstdT[:, :], pG[1][0:16, 0:128]), after=[ttr], tok=True)
                tB = {}
                for h in range(H):
                    pos = POS[h]
                    tB[h] = b.emit("tensor", lambda t, h=h, pos=pos: t.matmul(
                        pCB[:, pos * 128:(pos + 1) * 128],
                        per["sel16"][0:16, h * 128:(h + 1) * 128],
                        rstdT[0:16, :], start=True, stop=True),
                        after=[tte, tOTf] if h == 0 else (), tok=(h == H - 1))
                to1 = b.emit("vector", lambda v, c=c: v.scalar_tensor_tensor(
                    o1c[:, :].rearrange("p (q t) -> p q t", q=16),
                    rt(OTf, 16)[:, :, c * 128:(c + 1) * 128],
                    per["normw"][:, 0:1],
                    rt(silG, 16)[:, :, c * 128:(c + 1) * 128],
                    ALU.mult, ALU.mult), after=[tOTf, tsil], tok=True)
                togl = b.emit("vector", lambda v, c=c: v.tensor_tensor(
                    rt(og, 16)[:, :, c * 128:(c + 1) * 128],
                    o1c[:, :].rearrange("p (q t) -> p q t", q=16),
                    pCB[:, :].rearrange("p (q t) -> p q t", q=16), ALU.mult),
                    after=[to1, tB[H - 1], togdma], tok=True)
                tstats = togl
                tpg0, tpg1 = trl, tte
                tsilfree = to1
            togdma = b.emit("sync", lambda s, w=w: s.dma_start(
                out=io["sc_og"][w], in_=og[:, :]), after=[togl], dma="h_p3og")
        b.barrier()


def phase3b(b, io, per, windows=NW):
    nc = b.nc
    with ExitStack() as ph:
        pe_ = ph.enter_context
        wo_sb = pe_(nc.sbuf_tensor("p3b_wo", [128, NKT * D], BF16))
        gw_sb = pe_(nc.sbuf_tensor("p3b_gw", [128, NKT * E], BF16))
        gb_sb = pe_(nc.sbuf_tensor("p3b_gb", [E, 1], F32))
        gbias_sb = pe_(nc.sbuf_tensor("p3b_gbias", [E, 1], F32))
        og = pe_(nc.sbuf_tensor("p3b_og", [128, 16 * 512], BF16))
        xT = pe_(nc.sbuf_tensor("p3b_xT", [128, NKT * 512], F32))
        r1T = pe_(nc.sbuf_tensor("p3b_r1", [128, NKT * 512], F32))
        sq2 = og
        xn2T = pe_(nc.sbuf_tensor("p3b_xn2", [128, NKT * 512], BF16))
        rows = pe_(nc.sbuf_tensor("p3b_rows", [1, 4 * 512], F32))
        tmpS = pe_(nc.sbuf_tensor("p3b_tmpS", [128, 2 * 512], BF16))
        scs = pe_(nc.sbuf_tensor("p3b_scs", [8, 512], F32))
        scn = pe_(nc.sbuf_tensor("p3b_scn", [128, 8], F32))
        tcols = pe_(nc.sbuf_tensor("p3b_tcols", [128, 8], F32))
        comb = pe_(nc.sbuf_tensor("p3b_comb", [128, 8], F32))
        P = [pe_(nc.psum_tensor(f"p3b_P{i}", [128, 512], F32)) for i in range(6)]

        tw = b.emit("gpsimd", lambda g: g.dma_start(
            out=wo_sb[:, :].rearrange("p (k n) -> p k n", k=NKT),
            in_=io["wo"].rearrange("(k p) n -> p k n", p=128)), dma="g_p3bw")
        tw2 = b.emit("gpsimd", lambda g: g.dma_start(
            out=gw_sb[:, :].rearrange("p (k n) -> p k n", k=NKT),
            in_=io["gwT"].rearrange("(k p) n -> p k n", p=128)), dma="g_p3bw")
        tw3 = b.emit("gpsimd", lambda g: g.dma_start(
            out=gb_sb[:, :], in_=io["gb"].rearrange("o e -> e o")), dma="g_p3bw")
        tw4 = b.emit("gpsimd", lambda g: g.dma_start(
            out=gbias_sb[:, :], in_=io["gbias"].rearrange("o e -> e o")), dma="g_p3bw")
        tw = tw2 = tw3 = tw4

        tr1l = txn2l = tscl = tcombl = togr = tmm2l = tdr1l = txn2d = None
        for w in range(windows):
            tog = b.emit("sync", lambda s, w=w: s.dma_start(
                out=og[:, :], in_=io["sc_og"][w]), after=[togr, tmm2l], dma="h_p3bog")
            tx = b.emit("sync", lambda s, w=w: s.dma_start(
                out=rt(xT[:, :], NKT),
                in_=io["xT"][:, w * 512:(w + 1) * 512].rearrange("(k p) t -> p k t", p=128)),
                after=[tr1l], dma="h_p3bx")
            tr1 = {}
            trr = None
            for dblk in range(NKT):
                pp = P[dblk % 2]
                tO = None
                for vdt in range(NKT):
                    tO = b.emit("tensor", lambda t, dblk=dblk, vdt=vdt, pp=pp: t.matmul(
                        pp[:, :], wo_sb[:, vdt * D + dblk * 128: vdt * D + (dblk + 1) * 128],
                        og[:, POS[vdt] * 512:(POS[vdt] + 1) * 512],
                        start=(vdt == 0), stop=(vdt == NKT - 1)),
                        after=[tog, tw, tr1.get(dblk - 2)] if vdt == 0 else (),
                        tok=(vdt == NKT - 1))
                tr1[dblk] = b.emit("vector", lambda v, dblk=dblk, pp=pp: v.tensor_tensor(
                    r1T[:, dblk * 512:(dblk + 1) * 512], pp[:, :],
                    xT[:, dblk * 512:(dblk + 1) * 512], ALU.add),
                    after=[tO, tx, txn2l, tdr1l], tok=True)
            togr = tO
            tr1l = tr1[NKT - 1]
            tdr1l = b.emit("sync", lambda s, w=w: s.dma_start(
                out=io["sc_r1T"][w], in_=r1T[:, :]), after=[tr1l], dma="h_p3br1")
            tsq = b.emit("scalar", lambda s: s.activation(sq2[:, :], r1T[:, :], AF.Square),
                         after=[tr1l, tscl, togr], tok=True)
            mmt = mmt2 = None
            for k in range(NKT):
                mmt = b.emit("tensor", lambda t, k=k: t.matmul(
                    P[2][0:1, :], per["ones_colf"][:, :], r1T[:, k * 512:(k + 1) * 512],
                    start=(k == 0), stop=(k == NKT - 1)),
                    after=[tr1[k], tcombl] if k == 0 else [tr1[k]], tok=(k == NKT - 1))
            for k in range(NKT):
                mmt2 = b.emit("tensor", lambda t, k=k: t.matmul(
                    P[3][0:1, :], per["ones_col"][:, :],
                    sq2[:, k * 512:(k + 1) * 512],
                    start=(k == 0), stop=(k == NKT - 1)),
                    after=[tsq] if k == 0 else (), tok=(k == NKT - 1))
            t1 = b.emit("vector", lambda v: v.tensor_scalar_mul(
                rows[0:1, 0:512], P[2][0:1, :], 1.0 / D), after=[mmt], tok=True)
            tmm2l = mmt2
            t2 = b.emit("vector", lambda v: v.tensor_scalar_mul(
                rows[0:1, 512:1024], P[3][0:1, :], 1.0 / D), after=[mmt2], tok=True)
            t3 = b.emit("vector", lambda v: v.tensor_tensor(
                rows[0:1, 1024:1536], rows[0:1, 0:512], rows[0:1, 0:512], ALU.mult),
                after=[t1], tok=True)
            t4 = b.emit("vector", lambda v: v.tensor_tensor(
                rows[0:1, 1536:2048], rows[0:1, 512:1024], rows[0:1, 1024:1536], ALU.subtract),
                after=[t2, t3], tok=True)
            t5 = b.emit("scalar", lambda s: s.activation(
                rows[0:1, 512:1024], rows[0:1, 1536:2048], AF.Ln, bias=per["cbias"][0:1, 0:1]),
                after=[t4], tok=True)
            t6 = b.emit("scalar", lambda s: s.activation(
                rows[0:1, 1024:1536], rows[0:1, 512:1024], AF.Exp, scale=-0.5),
                after=[t5, t3], tok=True)
            t7 = b.emit("vector", lambda v: v.tensor_tensor(
                rows[0:1, 1536:2048], rows[0:1, 0:512], rows[0:1, 1024:1536], ALU.mult),
                after=[t6, t4], tok=True)
            tb1 = b.emit("tensor", lambda t: t.matmul(
                P[4][:, :], per["ones1f"][:, :], rows[0:1, 1024:1536], start=True, stop=True),
                after=[t6, txn2l], tok=True)
            tb2 = b.emit("tensor", lambda t: t.matmul(
                P[5][:, :], per["ones1f"][:, :], rows[0:1, 1536:2048], start=True, stop=True),
                after=[t7], tok=True)
            txn2 = None
            tprev = {0: None, 1: None}
            for k in range(NKT):
                par = k % 2
                ta = b.emit("vector", lambda v, k=k, par=par: v.tensor_tensor(
                    tmpS[:, par * 512:(par + 1) * 512], r1T[:, k * 512:(k + 1) * 512],
                    P[4][:, :], ALU.mult),
                    after=[tb1, mmt, mmt2, tprev[par]], tok=True)
                txn2 = b.emit("vector", lambda v, k=k, par=par: v.tensor_tensor(
                    xn2T[:, k * 512:(k + 1) * 512], tmpS[:, par * 512:(par + 1) * 512],
                    P[5][:, :], ALU.subtract), after=[tb2, ta, txn2d], tok=True)
                tprev[par] = txn2
            txn2l = txn2
            txn2d = b.emit("sync", lambda s, w=w: s.dma_start(
                out=io["sc_xn2T"][w], in_=xn2T[:, :]), after=[txn2], dma="h_p3bxn2")
            tsc = None
            for k in range(NKT):
                tsc = b.emit("tensor", lambda t, k=k: t.matmul(
                    P[2][0:8, :], gw_sb[:, k * E:(k + 1) * E],
                    xn2T[:, k * 512:(k + 1) * 512],
                    start=(k == 0), stop=(k == NKT - 1)),
                    after=[txn2, tw2, t1] if k == 0 else (), tok=(k == NKT - 1))
            tsce = b.emit("scalar", lambda s: s.copy(scs[:, :], P[2][0:8, :]),
                          after=[tsc], tok=True)
            tscl = b.emit("vector", lambda v: v.tensor_scalar(
                scs[:, :], scs[:, :], gb_sb[0:8, 0:1], gbias_sb[0:8, 0:1],
                ALU.add, ALU.add), after=[tsce, tw3, tw4], tok=True)
            tcomb = None
            for c in range(NCH):
                ttr = b.emit("tensor", lambda t, c=c: t.transpose(
                    P[3][0:128, 0:8],
                    scs[:, c * 128:(c + 1) * 128], per["identf"][0:8, 0:8]),
                    after=[tscl, tcomb], tok=True)
                tte = b.emit("vector", lambda v: v.tensor_copy(
                    scn[:, :], P[3][0:128, 0:8]), after=[ttr], tok=True)
                tm1 = b.emit("vector", lambda v: v.tensor_reduce(
                    tcols[:, 0:1], scn[:, :], mybir.AxisListType.X, ALU.max),
                    after=[tte], tok=True)
                te1 = b.emit("vector", lambda v: v.tensor_single_scalar(
                    comb[:, 0:8], scn[:, :], tcols[:, 0:1], ALU.is_equal),
                    after=[tm1], tok=True)
                tmsk = b.emit("vector", lambda v: v.scalar_tensor_tensor(
                    scn[:, :], comb[:, 0:8], -1e30, scn[:, :], ALU.mult, ALU.add),
                    after=[te1], tok=True)
                tm2 = b.emit("vector", lambda v: v.tensor_reduce(
                    tcols[:, 1:2], scn[:, :], mybir.AxisListType.X, ALU.max),
                    after=[tmsk], tok=True)
                te2 = b.emit("vector", lambda v: v.tensor_single_scalar(
                    scn[:, :], scn[:, :], tcols[:, 1:2], ALU.is_equal),
                    after=[tm2], tok=True)
                td = b.emit("vector", lambda v: v.tensor_tensor(
                    tcols[:, 2:3], tcols[:, 1:2], tcols[:, 0:1], ALU.subtract),
                    after=[te2], tok=True)
                te21 = b.emit("scalar", lambda s: s.activation(
                    tcols[:, 3:4], tcols[:, 2:3], AF.Exp), after=[td], tok=True)
                tden = b.emit("vector", lambda v: v.tensor_scalar_add(
                    tcols[:, 4:5], tcols[:, 3:4], 1.0), after=[te21], tok=True)
                tg1 = b.emit("vector", lambda v: v.reciprocal(
                    tcols[:, 5:6], tcols[:, 4:5]), after=[tden], tok=True)
                tg2 = b.emit("vector", lambda v: v.tensor_scalar(
                    tcols[:, 6:7], tcols[:, 5:6], -1.0, 1.0, ALU.mult, ALU.add),
                    after=[tg1], tok=True)
                tc1 = b.emit("vector", lambda v: v.tensor_scalar_mul(
                    comb[:, 0:8], comb[:, 0:8], tcols[:, 5:6]), after=[tg1], tok=True)
                tc2 = b.emit("vector", lambda v: v.scalar_tensor_tensor(
                    comb[:, 0:8], scn[:, :], tcols[:, 6:7], comb[:, 0:8],
                    ALU.mult, ALU.add), after=[tg2, tc1], tok=True)
                ttc = b.emit("tensor", lambda t, c=c: t.transpose(
                    P[3][0:8, 128:256], comb[:, 0:8], per["identf"][:, :]),
                    after=[tc2, tte], tok=True)
                tcomb = b.emit("scalar", lambda s, w=w, c=c: s.copy(
                    per["combT"][0:8, w * 512 + c * 128: w * 512 + (c + 1) * 128],
                    P[3][0:8, 128:256]), after=[ttc], tok=True)
            tcombl = tcomb
        b.barrier()


def phase_ffn(b, io, per, windows=NW):
    nc = b.nc
    with ExitStack() as ph:
        pe_ = ph.enter_context
        w1b = pe_(nc.sbuf_tensor("f_w1b", [128, 2 * NKT * HH], BF16))
        w2s = pe_(nc.sbuf_tensor("f_w2s", [128, 2 * NHT * 512], BF16))
        ht = pe_(nc.sbuf_tensor("f_ht", [128, 2 * NHT * 512], BF16))
        xn2T = pe_(nc.sbuf_tensor("f_xn2", [128, NKT * 512], BF16))
        yT = pe_(nc.sbuf_tensor("f_y", [128, NKT * 512], F32))
        cb = pe_(nc.sbuf_tensor("f_cb", [128, 2 * 512], BF16))
        htmp = pe_(nc.sbuf_tensor("f_htmp", [128, 2 * 512], BF16))
        b1r_sb = pe_(nc.sbuf_tensor("f_b1r", [128, E * NHT], F32))
        b2_sb = pe_(nc.sbuf_tensor("f_b2", [E, D], BF16))
        pH = [pe_(nc.psum_tensor(f"f_pH{i}", [128, 512], F32)) for i in range(2)]
        pY = [pe_(nc.psum_tensor(f"f_pY{i}", [128, 512], F32)) for i in range(4)]
        pC = pe_(nc.psum_tensor("f_pC", [128, 512], F32))

        tb1 = b.emit("sync", lambda s: s.dma_start(
            out=b1r_sb[:, :], in_=io["b1r"]), dma="h_fb")
        tb2l = b.emit("gpsimd", lambda g: g.dma_start(
            out=b2_sb[:, :], in_=io["b2"]), dma="g_fb")

        txl = None
        tsil = {}
        typrev = {}
        tydl = None
        tHlast = {0: None, 1: None}
        tcbl = {0: None, 1: None}
        for w in range(windows):
            tx = b.emit("sync", lambda s, w=w: s.dma_start(
                out=xn2T[:, :], in_=io["sc_xn2T"][w]), after=[txl], dma="h_fx")
            tyi = b.emit("sync", lambda s, w=w: s.dma_start(
                out=yT[:, :], in_=io["sc_r1T"][w]), after=[tydl], dma="h_fy")
            for rnd in range(4):
                tw1 = {}
                for je in range(2):
                    e_ = rnd * 2 + je
                    par = e_ % 2
                    tw1[je] = b.emit("gpsimd", lambda g, e_=e_, par=par: g.dma_start(
                        out=w1b[:, par * NKT * HH:(par + 1) * NKT * HH],
                        in_=io["sc_w1b"][e_]),
                        after=[tHlast[par]], dma="g_fw1")
                for je in range(2):
                    e_ = rnd * 2 + je
                    par = e_ % 2
                    tcb = b.emit("tensor", lambda t, e_=e_, w=w: t.matmul(
                        pC[:, 0:512], per["sel8"][0:8, e_ * 128:(e_ + 1) * 128],
                        per["combT"][0:8, w * 512:(w + 1) * 512],
                        start=True, stop=True), after=[tcbl[je], tx], tok=True)
                    tcbl[je] = b.emit("scalar", lambda s, je=je: s.copy(
                        cb[:, je * 512:(je + 1) * 512], pC[:, 0:512]),
                        after=[tcb, tsil.get((1 - je, NHT - 1)), tsil.get((je, NHT - 1))],
                        tok=True)
                    for mb in range(NHT):
                        pp = pH[mb % 2]
                        tH = None
                        xdep = tsil.get((1 - je, NHT - 1 if mb % 2 == (NHT - 1) % 2
                                          else NHT - 2)) if mb < 2 else None
                        for k in range(NKT):
                            tH = b.emit("tensor", lambda t, mb=mb, k=k, pp=pp, par=par: t.matmul(
                                pp[:, :],
                                w1b[:, par * NKT * HH + k * HH + mb * 128:
                                    par * NKT * HH + k * HH + (mb + 1) * 128],
                                xn2T[:, k * 512:(k + 1) * 512],
                                start=(k == 0), stop=(k == NKT - 1)),
                                after=[tw1[je], tx, tsil.get((je, mb - 2)), typrev.get(3), xdep]
                                if k == 0 else (), tok=(k == NKT - 1))
                        par2 = mb % 2
                        tse = b.emit("scalar", lambda s, mb=mb, e_=e_, pp=pp, par2=par2: s.activation(
                            htmp[:, par2 * 512:(par2 + 1) * 512],
                            pp[:, :], AF.Silu,
                            bias=b1r_sb[:, e_ * NHT + mb: e_ * NHT + mb + 1]),
                            after=[tH, tb1, tsil.get((je, mb - 2)), xdep], tok=True)
                        tsil[(je, mb)] = b.emit("vector", lambda v, mb=mb, je=je, par2=par2: v.tensor_tensor(
                            ht[:, (je * NHT + mb) * 512:(je * NHT + mb + 1) * 512],
                            htmp[:, par2 * 512:(par2 + 1) * 512],
                            cb[:, je * 512:(je + 1) * 512], ALU.mult),
                            after=[tse, tcbl[je]], tok=True)
                    tHlast[par] = tH
                thdone = tsil[(1, NHT - 1)]
                thdone0 = tsil[(0, NHT - 1)]
                tw2tok = {}
                tmm = {-1: None}
                for dgrp in range(4):
                    for je in range(2):
                        e_ = rnd * 2 + je
                        gi = dgrp * 2 + je
                        par = gi % 2
                        tw2tok[gi] = b.emit("gpsimd", lambda g, e_=e_, dgrp=dgrp, par=par: g.dma_start(
                            out=w2s[:, par * NHT * 512:(par + 1) * NHT * 512]
                                .rearrange("p (k n) -> p k n", k=NHT),
                            in_=io["sc_w2b"][e_].rearrange("p (k n) -> p k n", k=NHT)
                                [:, :, dgrp * 512:(dgrp + 1) * 512]),
                            after=[tmm.get(gi - 2), tb2l], dma="g_fw2")
                    for je in range(2):
                        e_ = rnd * 2 + je
                        gi = dgrp * 2 + je
                        par = gi % 2
                        for dblk in range(4):
                            d_ = dgrp * 4 + dblk
                            if je == 0 and rnd == 0:
                                b.emit("tensor", lambda t, d_=d_, dblk=dblk, w=w: t.matmul(
                                    pY[dblk][:, :],
                                    b2_sb[0:8, d_ * 128:(d_ + 1) * 128],
                                    per["combT"][0:8, w * 512:(w + 1) * 512],
                                    start=True, stop=False),
                                    after=[thdone, thdone0, typrev.get(dblk)])
                            for kth in range(NHT):
                                last = (je == 1 and kth == NHT - 1)
                                jelast = (dblk == 3 and kth == NHT - 1)
                                first = (je == 0 and kth == 0 and rnd > 0)
                                tmm[gi] = b.emit("tensor", lambda t, je=je, kth=kth, dblk=dblk, par=par, first=first: t.matmul(
                                    pY[dblk][:, :],
                                    w2s[:, par * NHT * 512 + kth * 512 + dblk * 128:
                                        par * NHT * 512 + kth * 512 + (dblk + 1) * 128],
                                    ht[:, (je * NHT + kth) * 512:(je * NHT + kth + 1) * 512],
                                    start=first, stop=last),
                                    after=[tw2tok[gi], thdone, thdone0, typrev.get(dblk)]
                                    if kth == 0 and dblk == 0 else (),
                                    tok=jelast)
                    for dblk in range(4):
                        d_ = dgrp * 4 + dblk
                        tya = b.emit("vector", lambda v, d_=d_, dblk=dblk: v.tensor_tensor(
                            yT[:, d_ * 512:(d_ + 1) * 512], pY[dblk][:, :],
                            yT[:, d_ * 512:(d_ + 1) * 512], ALU.add),
                            after=[tmm[dgrp * 2 + 1], tyi], tok=True)
                        typrev[dblk] = tya
            tydl = b.emit("sync", lambda s, w=w: s.dma_start(
                out=io["outT"][:, w * 512:(w + 1) * 512].rearrange("(k p) t -> p k t", p=128),
                in_=rt(yT[:, :], NKT)), after=[typrev[0], typrev[1], typrev[2], typrev[3]],
                dma="h_fo")
            txl = tx
        if os.environ.get("T_FDBG"):
            dht = nc.dram_tensor("dbg_ht", [128, 2 * NHT * 512], BF16, kind="ExternalOutput").ap()
            dcb = nc.dram_tensor("dbg_cb", [128, 1024], BF16, kind="ExternalOutput").ap()
            toks = b.alltoks()
            b.waitonly("sync", toks)
            b.emit("sync", lambda s: s.dma_start(out=dht, in_=ht[:, :]), dma="h_fdbg")
            b.emit("sync", lambda s: s.dma_start(out=dcb, in_=cb[:, :]), dma="h_fdbg")
        b.barrier()


def build_nc(phases=("p1",), dbg=(), p1_windows=NW,
             p1_stages=("ln", "gate", "gate2", "gate3", "gate4", "proj")):
    nc = bass.Bass("TRN2", target_bir_lowering=False)
    io = {}
    io["xT"] = nc.dram_tensor("xT", [D, TC], F32, kind="ExternalInput").ap()
    io["wq"] = nc.dram_tensor("wq", [D, KD], F32, kind="ExternalInput").ap()
    io["wk"] = nc.dram_tensor("wk", [D, KD], F32, kind="ExternalInput").ap()
    io["wv"] = nc.dram_tensor("wv", [D, VD], F32, kind="ExternalInput").ap()
    io["wg"] = nc.dram_tensor("wg", [D, VD], F32, kind="ExternalInput").ap()
    io["wo"] = nc.dram_tensor("wo", [VD, D], F32, kind="ExternalInput").ap()
    io["wgk1"] = nc.dram_tensor("wgk1", [D, LR], F32, kind="ExternalInput").ap()
    io["wgk2"] = nc.dram_tensor("wgk2", [LR, KD], F32, kind="ExternalInput").ap()
    io["bgk2"] = nc.dram_tensor("bgk2", [1, KD], F32, kind="ExternalInput").ap()
    io["normw"] = nc.dram_tensor("normw", [DV, 1], F32, kind="ExternalInput").ap()
    io["gwT"] = nc.dram_tensor("gwT", [D, E], F32, kind="ExternalInput").ap()
    io["gb"] = nc.dram_tensor("gb", [1, E], F32, kind="ExternalInput").ap()
    io["gbias"] = nc.dram_tensor("gbias", [1, E], F32, kind="ExternalInput").ap()
    io["w1T"] = nc.dram_tensor("w1T", [E, D, HH], F32, kind="ExternalInput").ap()
    io["b1r"] = nc.dram_tensor("b1r", [128, E * NHT], F32, kind="ExternalInput").ap()
    io["w2T"] = nc.dram_tensor("w2T", [E, HH, D], F32, kind="ExternalInput").ap()
    io["b2"] = nc.dram_tensor("b2", [E, D], F32, kind="ExternalInput").ap()
    io["smask"] = nc.dram_tensor("smask", [128, 1], F32, kind="ExternalInput").ap()
    io["outT"] = nc.dram_tensor("outT", [D, TC], F32, kind="ExternalOutput").ap()

    io["sc_xnT"] = [nc.dram_tensor(f"sc_xnT{w}", [128, NKT * 512], BF16).ap() for w in range(NW)]
    io["sc_qtT"] = [nc.dram_tensor(f"sc_qtT{w}", [128, NQT * 512], BF16).ap() for w in range(NW)]
    io["sc_ktT"] = [nc.dram_tensor(f"sc_ktT{w}", [128, NQT * 512], BF16).ap() for w in range(NW)]
    io["sc_OT"] = [nc.dram_tensor(f"sc_OT{w}", [128, NVT * 512], BF16).ap() for w in range(NW)]
    io["sc_xn2T"] = [nc.dram_tensor(f"sc_xn2T{w}", [128, NKT * 512], BF16).ap() for w in range(NW)]
    io["sc_r1T"] = [nc.dram_tensor(f"sc_r1T{w}", [128, NKT * 512], F32).ap() for w in range(NW)]
    io["sc_og"] = [nc.dram_tensor(f"sc_og{w}", [128, NVT * 512], BF16).ap() for w in range(NW)]
    io["sc_w1b"] = [nc.dram_tensor(f"sc_w1b{e_}", [128, NKT * HH], BF16).ap() for e_ in range(E)]
    io["sc_w2b"] = [nc.dram_tensor(f"sc_w2b{e_}", [128, NHT * D], BF16).ap() for e_ in range(E)]
    io["cc_in"] = nc.dram_tensor("cc_in", [128, NQT * 128], F32)
    io["cc_out"] = nc.dram_tensor("cc_out", [256, NQT * 128], F32)

    cmask_np = np.triu(np.ones((128, 128), dtype=np.float32))
    io["c_cmask"] = nc.inline_tensor(cmask_np, name="c_cmask")
    io["c_ident"] = nc.inline_tensor(np.eye(128, dtype=BF), name="c_ident")
    io["c_identf"] = nc.inline_tensor(np.eye(128, dtype=np.float32), name="c_identf")
    sel16 = np.zeros((16, 16 * 128), dtype=np.float32)
    for h in range(16):
        sel16[h, h * 128:(h + 1) * 128] = 1.0
    io["c_sel16"] = nc.inline_tensor(sel16, name="c_sel16")
    sel8 = np.zeros((8, 8 * 128), dtype=ml_dtypes.bfloat16)
    for e_ in range(8):
        sel8[e_, e_ * 128:(e_ + 1) * 128] = 1.0
    io["c_sel8"] = nc.inline_tensor(sel8, name="c_sel8")
    io["c_ones_col"] = nc.inline_tensor(np.ones((128, 1), dtype=BF), name="c_ones_col")
    io["c_ones_colf"] = nc.inline_tensor(np.ones((128, 1), dtype=np.float32), name="c_ones_colf")
    io["c_ones1f"] = nc.inline_tensor(np.ones((1, 128), dtype=np.float32), name="c_ones1f")
    io["c_onesrow"] = nc.inline_tensor(np.ones((1, 512), dtype=BF), name="c_onesrow")

    with ExitStack() as ctx:
        e = ctx.enter_context
        b = Bld(nc, ctx)
        per = {}
        per["zeros"] = e(nc.sbuf_tensor("g_zeros", [128, 512], BF16))
        per["ones_col"] = e(nc.sbuf_tensor("g_ones_col", [128, 1], BF16))
        per["ones_colf"] = e(nc.sbuf_tensor("g_ones_colf", [128, 1], F32))
        per["cbias"] = e(nc.sbuf_tensor("g_cbias", [128, 4], F32))
        per["ones1f"] = e(nc.sbuf_tensor("g_ones1f", [1, 128], F32))
        per["onesrow_bf"] = e(nc.sbuf_tensor("g_onesrow", [1, 512], BF16))
        per["cmask"] = e(nc.sbuf_tensor("g_cmask", [128, 128], F32))
        per["ident"] = e(nc.sbuf_tensor("g_ident", [128, 128], BF16))
        per["identf"] = e(nc.sbuf_tensor("g_identf", [128, 128], F32))
        per["sel8"] = e(nc.sbuf_tensor("g_sel8", [8, 8 * 128], BF16))
        per["expaC"] = e(nc.sbuf_tensor("g_expaC", [128, NG * 8], F32))
        per["lam"] = e(nc.sbuf_tensor("g_lam", [128, NG * 8], F32))
        per["lamrun"] = e(nc.sbuf_tensor("g_lamrun", [128, 8], F32))
        per["normw"] = e(nc.sbuf_tensor("g_normw", [DV, 1], F32))

        tz = b.emit("vector", lambda v: v.memset(per["zeros"][:, :], 0.0), tok=True)
        b.emit("vector", lambda v: v.memset(per["cbias"][:, 0:1], EPS))
        b.emit("vector", lambda v: v.memset(per["cbias"][:, 1:2], 1.0))
        tz2 = b.emit("vector", lambda v: v.memset(per["cbias"][:, 2:3], LN8), tok=True)
        b.emit("sync", lambda s: s.dma_start(out=per["ones_col"][:, :], in_=io["c_ones_col"].ap()), dma="h_const")
        b.emit("sync", lambda s: s.dma_start(out=per["ones_colf"][:, :], in_=io["c_ones_colf"].ap()), dma="h_const")
        b.emit("sync", lambda s: s.dma_start(out=per["ones1f"][:, :], in_=io["c_ones1f"].ap()), dma="h_const")
        b.emit("sync", lambda s: s.dma_start(out=per["onesrow_bf"][:, :], in_=io["c_onesrow"].ap()), dma="h_const")
        b.emit("sync", lambda s: s.dma_start(out=per["cmask"][:, :], in_=io["c_cmask"].ap()), dma="h_const")
        b.emit("sync", lambda s: s.dma_start(out=per["identf"][:, :], in_=io["c_identf"].ap()), dma="h_const")
        b.emit("sync", lambda s: s.dma_start(out=per["sel8"][:, :], in_=io["c_sel8"].ap()), dma="h_const")
        b.emit("sync", lambda s: s.dma_start(out=per["ident"][:, :], in_=io["c_ident"].ap()), dma="h_const")
        tld = b.emit("sync", lambda s: s.dma_start(out=per["normw"][:, :], in_=io["normw"]), dma="h_const")
        b.waitonly("tensor", [tld, tz])
        b.waitonly("vector", [tld])
        b.waitonly("scalar", [tld, tz2])
        b.waitonly("gpsimd", [tld])

        if "p1" in phases:
            phase1(b, io, per, windows=p1_windows, stages=p1_stages)
        if "p2" in phases:
            phase2(b, io, per)
        if "p3a" in phases:
            per["combT"] = e(nc.sbuf_tensor("g_combT", [E, TC], BF16))
        if "cc" in phases or "p3a" in phases:
            with ExitStack() as s3:
                per["Sin"] = s3.enter_context(nc.sbuf_tensor("g_Sin", [128, NQT * 128], F32))
                per["Sinb"] = s3.enter_context(nc.sbuf_tensor("g_Sinb", [128, NQT * 128], BF16))
                per["smask"] = s3.enter_context(nc.sbuf_tensor("g_smask", [128, 1], F32))
                per["sel16"] = s3.enter_context(nc.sbuf_tensor("g_sel16", [16, 16 * 128], F32))
                b.emit("sync", lambda s: s.dma_start(
                    out=per["sel16"][:, :], in_=io["c_sel16"].ap()), dma="h_const2")
                if "cc" in phases:
                    phase_cc(b, io, per)
                if "p3a" in phases:
                    phase3a(b, io, per)
        if "p3b" in phases:
            phase3b(b, io, per)
        if "ffn" in phases:
            phase_ffn(b, io, per)

        for name in dbg:
            if name == "cc_in":
                dout = nc.dram_tensor("dbg_cc_in", [128, NQT * 128], F32,
                                      kind="ExternalOutput").ap()
                b.emit("sync", lambda s, dout=dout: s.dma_start(
                    out=dout, in_=io["cc_in"].ap()), dma="h_dbg")
            elif name in ("sc_xnT", "sc_qtT", "sc_ktT", "sc_OT", "sc_og", "sc_xn2T", "sc_r1T"):
                shp = io[name][0].shape
                dt = F32 if name == "sc_r1T" else BF16
                dout = nc.dram_tensor("dbg_" + name, [NW, shp[0], shp[1]], dt,
                                      kind="ExternalOutput").ap()
                for w in range(NW):
                    b.emit("sync", lambda s, w=w, dout=dout, name=name: s.dma_start(
                        out=dout[w], in_=io[name][w]), dma="h_dbg")
            elif name == "combT":
                dout = nc.dram_tensor("dbg_combT", [E, TC], F32, kind="ExternalOutput").ap()
                dtmp = nc.dram_tensor("dbg_combT_i", [E, TC], BF16)
                b.emit("sync", lambda s, dtmp=dtmp: s.dma_start(
                    out=dtmp.ap(), in_=per["combT"][:, :]), dma="h_dbg")
                b.emit("gpsimd", lambda g, dout=dout, dtmp=dtmp: g.dma_start(
                    out=dout, in_=dtmp.ap()), after=[("h_dbg", b.cnt.get("h_dbg", 0))], dma="g_dbg")
            elif name in ("expaC", "lam"):
                dout = nc.dram_tensor("dbg_" + name, [128, NG * 8], F32,
                                      kind="ExternalOutput").ap()
                b.emit("sync", lambda s, dout=dout, name=name: s.dma_start(
                    out=dout, in_=per[name][:, :]), dma="h_dbg")

        if "ffn" not in phases:
            b.emit("gpsimd", lambda g: g.dma_start(
                out=io["outT"][0:128, 0:512], in_=per["zeros"][:, :]), dma="g_fin")
        b.waitonly("sync", b.alltoks())

        blk = e(nc.Block())
        b.replay(blk)
    return nc


def host_prep(inputs):
    x = inputs["L0_attn_x"]
    f32 = np.float32
    com = dict(
        wq=np.ascontiguousarray(inputs["L0_attn_Wq"], f32),
        wk=np.ascontiguousarray(inputs["L0_attn_Wk"], f32),
        wv=np.ascontiguousarray(inputs["L0_attn_Wv"], f32),
        wg=np.ascontiguousarray(inputs["L0_attn_Wg"], f32),
        wo=np.ascontiguousarray(inputs["L0_attn_Wo"], f32),
        wgk1=np.ascontiguousarray(inputs["L0_attn_Wgk1"], f32),
        wgk2=np.ascontiguousarray(inputs["L0_attn_Wgk2"], f32),
        bgk2=np.ascontiguousarray(inputs["L0_attn_bgk2"].reshape(1, KD), f32),
        normw=np.ascontiguousarray(inputs["L0_attn_norm_w"].reshape(DV, 1), f32),
        gwT=np.ascontiguousarray(inputs["L0_ffn_gate_w"].T, f32),
        gb=np.ascontiguousarray(inputs["L0_ffn_gate_b"].reshape(1, E), f32),
        gbias=np.ascontiguousarray(inputs["L0_ffn_bias"].reshape(1, E), f32),
        w1T=np.ascontiguousarray(inputs["L0_ffn_w1"].transpose(0, 2, 1), f32),
        b1r=np.ascontiguousarray(
            inputs["L0_ffn_b1"].reshape(E, NHT, 128).transpose(2, 0, 1).reshape(128, E * NHT), f32),
        w2T=np.ascontiguousarray(inputs["L0_ffn_w2"].transpose(0, 2, 1), f32),
        b2=np.ascontiguousarray(inputs["L0_ffn_b2"], f32),
    )
    in_maps = []
    for core in range(8):
        bidx, half = core // 2, core % 2
        m = dict(com)
        m["xT"] = np.ascontiguousarray(x[bidx, half * TC:(half + 1) * TC, :].T, f32)
        m["smask"] = np.full((128, 1), float(half), f32)
        in_maps.append(m)
    return in_maps


_NC_CACHE = {}


def _get_nc(phases, dbg=(), **kw):
    key = (tuple(phases), tuple(dbg), tuple(sorted(kw.items())))
    if key not in _NC_CACHE:
        _NC_CACHE[key] = build_nc(phases, dbg, **kw)
    return _NC_CACHE[key]


def run_dev(inputs, phases=("p1",), dbg=(), **kw):
    nc = _get_nc(phases, dbg, **kw)
    in_maps = host_prep(inputs)
    res = run_bass_kernel_spmd(nc, in_maps, core_ids=list(range(8)))
    return res


def kernel(**inputs):
    res = run_dev(inputs, phases=("p1", "p2", "cc", "p3a", "p3b", "ffn"))
    out = np.zeros((4, 8192, D), np.float32)
    for core in range(8):
        bidx, half = core // 2, core % 2
        out[bidx, half * TC:(half + 1) * TC, :] = res.results[core]["outT"].T
    return out


# version 07
# speedup vs baseline: 1.0840x; 1.0840x over previous
import numpy as np
import ml_dtypes
from contextlib import ExitStack

import os
import concourse.bass as bass
import concourse.mybir as mybir
from concourse.bass_utils import run_bass_kernel_spmd

F32 = mybir.dt.float32
BF16 = mybir.dt.bfloat16
AF = mybir.ActivationFunctionType
ALU = mybir.AluOpType

D, KD, VD, H, DK, DV, LR = 2048, 1024, 2048, 16, 64, 128, 16
TC, W, C = 4096, 512, 128
NW, NCH, NG = TC // W, W // C, TC // C
NKT, NQT, NVT = D // 128, KD // 128, VD // 128
E, HH = 8, 1408
NHT = HH // 128
EPS = 1e-5
LN8 = float(np.log(DK ** -0.5))
BF = ml_dtypes.bfloat16


class Bld:

    ENGS = ("sync", "gpsimd", "scalar", "vector", "tensor")

    def __init__(self, nc, ctx):
        self.nc = nc
        self.ctx = ctx
        self.q = {e: [] for e in self.ENGS}
        self.h = {}
        self.cnt = {}
        for s in self.ENGS:
            self.h[s] = ctx.enter_context(nc.semaphore("S_" + s))
            self.cnt[s] = 0
        self.lastwait = {e: {} for e in self.ENGS}

    def dsem(self, name):
        if name not in self.h:
            self.h[name] = self.ctx.enter_context(self.nc.semaphore("S_" + name))
            self.cnt[name] = 0
        return name

    def _waits(self, eng, after):
        best = {}
        for t in after:
            if t is None:
                continue
            s, v = t
            if v > best.get(s, 0):
                best[s] = v
        waits = []
        for s, v in best.items():
            if v > self.lastwait[eng].get(s, 0):
                waits.append((s, v))
                self.lastwait[eng][s] = v
        return waits

    def emit(self, eng, fn, after=(), tok=False, dma=None):
        waits = self._waits(eng, after)
        inc = None
        if dma:
            s = self.dsem(dma)
            self.cnt[s] += 16
            inc = (s, 16)
        elif tok:
            self.cnt[eng] += 1
            inc = (eng, 1)
        Hh = self.h

        def run(engine, waits=waits, inc=inc, fn=fn):
            for s, v in waits:
                engine.wait_ge(Hh[s], v)
            ins = fn(engine)
            if inc is not None:
                ins.then_inc(Hh[inc[0]], inc[1])

        self.q[eng].append(run)
        return (inc[0], self.cnt[inc[0]]) if inc else None

    def waitonly(self, eng, after):
        waits = self._waits(eng, after)
        if not waits:
            return
        Hh = self.h

        def run(engine, waits=waits):
            for s, v in waits:
                engine.wait_ge(Hh[s], v)

        self.q[eng].append(run)

    def alltoks(self):
        return [(s, self.cnt[s]) for s in self.cnt if self.cnt[s] > 0]

    def barrier(self, engs=None):
        toks = self.alltoks()
        for e in (engs or self.ENGS):
            self.waitonly(e, toks)

    def replay(self, blk):
        for name in self.ENGS:
            steps = self.q[name]
            getattr(blk, name)(lambda eng, steps=steps: [f(eng) for f in steps])


def rt(ap, kt):
    return ap.rearrange("p (k t) -> p k t", k=kt)


def phase1(b, io, per, windows=NW, stages=None):
    nc = b.nc
    with ExitStack() as ph:
        pe_ = ph.enter_context
        wq_sb = pe_(nc.sbuf_tensor("p1_wq", [128, NKT * 1024], BF16))
        wk_sb = pe_(nc.sbuf_tensor("p1_wk", [128, NKT * 1024], BF16))
        wgk1_sb = pe_(nc.sbuf_tensor("p1_wgk1", [128, NKT * 16], BF16))
        wgk2_sb = pe_(nc.sbuf_tensor("p1_wgk2", [16, KD], BF16))
        bgk2_sb = pe_(nc.sbuf_tensor("p1_bgk2", [1, KD], BF16))
        xT = pe_(nc.sbuf_tensor("p1_xT", [128, NKT * 512], BF16))
        tmpS = pe_(nc.sbuf_tensor("p1_tmpS", [128, 2 * 512], BF16))
        xnT = pe_(nc.sbuf_tensor("p1_xnT", [128, NKT * 512], BF16))
        rows = pe_(nc.sbuf_tensor("p1_rows", [1, 4 * 512], F32))
        gk1 = pe_(nc.sbuf_tensor("p1_gk1", [16, 512], BF16))
        uB = pe_(nc.sbuf_tensor("p1_u", [128, 512], BF16))
        euB = pe_(nc.sbuf_tensor("p1_eu", [128, 512], BF16))
        l1B = pe_(nc.sbuf_tensor("p1_l1", [128, 512], BF16))
        rB = pe_(nc.sbuf_tensor("p1_r", [128, 512], BF16))
        T1T = pe_(nc.sbuf_tensor("p1_T1T", [128, NQT * 512], BF16))
        A_sb = pe_(nc.sbuf_tensor("p1_A", [128, NQT * 512], F32))
        sq = A_sb[:, :].bitcast(BF16)
        expq = pe_(nc.sbuf_tensor("p1_expq", [128, NQT * 512], BF16))
        expk = pe_(nc.sbuf_tensor("p1_expk", [128, NQT * 512], BF16))
        bq = pe_(nc.sbuf_tensor("p1_bq", [128, NQT * NCH], F32))
        bk = pe_(nc.sbuf_tensor("p1_bk", [128, NQT * NCH], F32))
        dcol = pe_(nc.sbuf_tensor("p1_dcol", [128, 8], F32))
        Qraw = pe_(nc.sbuf_tensor("p1_Qraw", [128, NQT * 512], BF16))
        Kraw = pe_(nc.sbuf_tensor("p1_Kraw", [128, NQT * 512], BF16))
        qt = pe_(nc.sbuf_tensor("p1_qt", [128, NQT * 512], BF16))
        kt_sb = pe_(nc.sbuf_tensor("p1_kt", [128, NQT * 512], BF16))
        P = [pe_(nc.psum_tensor(f"p1_P{i}", [128, 512], F32)) for i in range(6)]

        zeros = per["zeros"]
        ones_col = per["ones_col"]
        ones1f = per["ones1f"]
        onesrow_bf = per["onesrow_bf"]

        tw = []
        tw.append(b.emit("gpsimd", lambda g: g.dma_start(
            out=wq_sb[:, :].rearrange("p (k n) -> p k n", k=NKT),
            in_=io["wq"].rearrange("(k p) n -> p k n", p=128)), dma="g_p1w"))
        tw.append(b.emit("gpsimd", lambda g: g.dma_start(
            out=wk_sb[:, :].rearrange("p (k n) -> p k n", k=NKT),
            in_=io["wk"].rearrange("(k p) n -> p k n", p=128)), dma="g_p1w"))
        tw.append(b.emit("gpsimd", lambda g: g.dma_start(
            out=wgk1_sb[:, :].rearrange("p (k n) -> p k n", k=NKT),
            in_=io["wgk1"].rearrange("(k p) n -> p k n", p=128)), dma="g_p1w"))
        tw.append(b.emit("gpsimd", lambda g: g.dma_start(
            out=wgk2_sb[:, :], in_=io["wgk2"]), dma="g_p1w"))
        tw.append(b.emit("gpsimd", lambda g: g.dma_start(
            out=bgk2_sb[:, :], in_=io["bgk2"]), dma="g_p1w"))
        wtok = tw[-1]

        tlam = b.emit("vector", lambda v: v.memset(per["lamrun"][:, :], 1.0), tok=True)

        tq_done = tk_done = txn = None
        tqmul = tkmul = None
        texpq_l = texpk_l = None
        for w in range(windows):
            tx = b.emit("gpsimd", lambda g, w=w: g.dma_start(
                out=rt(xT[:, :], NKT),
                in_=io["xT"][:, w * 512:(w + 1) * 512].rearrange("(k p) t -> p k t", p=128)),
                after=[txn], dma="g_p1x")
            tsq = b.emit("scalar", lambda s: s.activation(sq, xT[:, :], AF.Square),
                         after=[tx, tkmul], tok=True)
            mmt = mmt2 = None
            for k in range(NKT):
                mmt = b.emit("tensor", lambda t, k=k: t.matmul(
                    P[0][0:1, :], ones_col[:, :], xT[:, k * 512:(k + 1) * 512],
                    start=(k == 0), stop=(k == NKT - 1)),
                    after=[tx, tq_done, tk_done] if k == 0 else (), tok=(k == NKT - 1))
            for k in range(NKT):
                mmt2 = b.emit("tensor", lambda t, k=k: t.matmul(
                    P[1][0:1, :], ones_col[:, :],
                    sq.rearrange("p (k t) -> p k t", k=NKT)[:, k, :],
                    start=(k == 0), stop=(k == NKT - 1)),
                    after=[tsq] if k == 0 else (), tok=(k == NKT - 1))
            t1 = b.emit("vector", lambda v: v.tensor_scalar_mul(
                rows[0:1, 0:512], P[0][0:1, :], 1.0 / D), after=[mmt], tok=True)
            t2 = b.emit("vector", lambda v: v.tensor_scalar_mul(
                rows[0:1, 512:1024], P[1][0:1, :], 1.0 / D), after=[mmt2], tok=True)
            t3 = b.emit("vector", lambda v: v.tensor_tensor(
                rows[0:1, 1024:1536], rows[0:1, 0:512], rows[0:1, 0:512], ALU.mult),
                after=[t1], tok=True)
            t4 = b.emit("vector", lambda v: v.tensor_tensor(
                rows[0:1, 1536:2048], rows[0:1, 512:1024], rows[0:1, 1024:1536], ALU.subtract),
                after=[t2, t3], tok=True)
            t5 = b.emit("scalar", lambda s: s.activation(
                rows[0:1, 512:1024], rows[0:1, 1536:2048], AF.Ln, bias=per["cbias"][0:1, 0:1]),
                after=[t4], tok=True)
            t6 = b.emit("scalar", lambda s: s.activation(
                rows[0:1, 1024:1536], rows[0:1, 512:1024], AF.Exp, scale=-0.5),
                after=[t5, t3], tok=True)
            t7 = b.emit("vector", lambda v: v.tensor_tensor(
                rows[0:1, 1536:2048], rows[0:1, 0:512], rows[0:1, 1024:1536], ALU.mult),
                after=[t6, t4], tok=True)
            tb1 = b.emit("tensor", lambda t: t.matmul(
                P[2][:, :], ones1f[:, :], rows[0:1, 1024:1536], start=True, stop=True),
                after=[t6, txn], tok=True)
            tb2 = b.emit("tensor", lambda t: t.matmul(
                P[3][:, :], ones1f[:, :], rows[0:1, 1536:2048], start=True, stop=True),
                after=[t7], tok=True)
            txn = None
            tprev = {0: None, 1: None}
            for k in range(NKT):
                par = k % 2
                ta = b.emit("vector", lambda v, k=k, par=par: v.tensor_tensor(
                    tmpS[:, par * 512:(par + 1) * 512], xT[:, k * 512:(k + 1) * 512],
                    P[2][:, :], ALU.mult),
                    after=[tb1, mmt, mmt2, tprev[par]], tok=True)
                txn = b.emit("vector", lambda v, k=k, par=par: v.tensor_tensor(
                    xnT[:, k * 512:(k + 1) * 512], tmpS[:, par * 512:(par + 1) * 512],
                    P[3][:, :], ALU.subtract), after=[tb2, ta], tok=True)
                tprev[par] = txn
            txnout = b.emit("sync", lambda s, w=w: s.dma_start(
                out=io["sc_xnT"][w], in_=xnT[:, :]), after=[txn], dma="h_p1xn")

            tg = None
            for k in range(NKT):
                tg = b.emit("tensor", lambda t, k=k: t.matmul(
                    P[4][0:16, :], wgk1_sb[:, k * 16:(k + 1) * 16],
                    xnT[:, k * 512:(k + 1) * 512],
                    start=(k == 0), stop=(k == NKT - 1)),
                    after=[txn, wtok] if k == 0 else (), tok=(k == NKT - 1))
            tg1 = b.emit("scalar", lambda s: s.copy(gk1[:, :], P[4][0:16, :]),
                         after=[tg], tok=True)

            tT1 = {}
            glog_state = dict(bank_free={0: tg1, 1: None}, peu=None, pl1=None,
                              pT1=None, pr=None)

            def emit_glog(mb):
                st = glog_state
                pp = P[4 + (mb % 2)]
                b.emit("tensor", lambda t, mb=mb, pp=pp: t.matmul(
                    pp[:, :], wgk2_sb[0:16, mb * 128:(mb + 1) * 128], gk1[:, :],
                    start=True, stop=False),
                    after=[tg1, st["bank_free"][mb % 2]])
                tm = b.emit("tensor", lambda t, mb=mb, pp=pp: t.matmul(
                    pp[:, :], bgk2_sb[0:1, mb * 128:(mb + 1) * 128], onesrow_bf[:, :],
                    start=False, stop=True), tok=True)
                tu = b.emit("scalar", lambda s, pp=pp: s.activation(
                    uB[:, :], pp[:, :], AF.Abs), after=[tm, st["peu"], st["pr"]], tok=True)
                teu = b.emit("scalar", lambda s: s.activation(
                    euB[:, :], uB[:, :], AF.Exp, scale=-1.0), after=[tu, st["pl1"]], tok=True)
                tl1 = b.emit("scalar", lambda s: s.activation(
                    l1B[:, :], euB[:, :], AF.Ln, bias=per["cbias"][:, 1:2]),
                    after=[teu, st["pT1"]], tok=True)
                tr = b.emit("vector", lambda v, pp=pp: v.scalar_tensor_tensor(
                    rB[:, :], uB[:, :], 1.0, pp[:, :], ALU.mult, ALU.subtract),
                    after=[tm, tu, st["pT1"]], tok=True)
                tT1[mb] = b.emit("vector", lambda v, mb=mb: v.scalar_tensor_tensor(
                    T1T[:, mb * 512:(mb + 1) * 512], rB[:, :], 0.5, l1B[:, :],
                    ALU.mult, ALU.add), after=[tl1, tr], tok=True)
                st["bank_free"][mb % 2] = tT1[mb]
                st["peu"], st["pl1"], st["pT1"], st["pr"] = teu, tl1, tT1[mb], tr

            tcpq = {}
            tcpk = {}
            tcp_order = []

            def emit_pair(which, pair, wsb, raw, copies):
                pA = P[(pair % 2) * 2]
                pB = P[(pair % 2) * 2 + 1]
                war = copies.get(pair - 2)
                pmm = {}
                for half, pp in ((0, pA), (1, pB)):
                    mb = pair * 2 + half
                    tmm = None
                    for k in range(NKT):
                        tmm = b.emit("tensor", lambda t, mb=mb, k=k, pp=pp, wsb=wsb: t.matmul(
                            pp[:, :], wsb[:, k * 1024 + mb * 128: k * 1024 + (mb + 1) * 128],
                            xnT[:, k * 512:(k + 1) * 512],
                            start=(k == 0), stop=(k == NKT - 1)),
                            after=[txn, wtok, war] if k == 0 else (),
                            tok=(k == NKT - 1))
                    pmm[half] = tmm
                for half, pp in ((0, pA), (1, pB)):
                    mb = pair * 2 + half
                    copies[pair] = b.emit("scalar", lambda s, mb=mb, pp=pp, raw=raw: s.copy(
                        raw[:, mb * 512:(mb + 1) * 512], pp[:, :]),
                        after=[pmm[half]], tok=True)

            emit_glog(0)
            emit_glog(1)
            emit_pair("q", 0, wq_sb, Qraw, tcpq)
            emit_glog(2)
            emit_pair("q", 1, wq_sb, Qraw, tcpq)
            emit_glog(3)
            emit_pair("q", 2, wq_sb, Qraw, tcpq)
            emit_glog(4)
            emit_pair("q", 3, wq_sb, Qraw, tcpq)
            emit_glog(5)
            emit_pair("k", 0, wk_sb, Kraw, tcpk)
            emit_glog(6)
            emit_pair("k", 1, wk_sb, Kraw, tcpk)
            emit_glog(7)
            emit_pair("k", 2, wk_sb, Kraw, tcpk)
            emit_pair("k", 3, wk_sb, Kraw, tcpk)

            tA = []
            for k in range(NQT):
                tA.append(b.emit("vector", lambda v, k=k: v.tensor_tensor_scan(
                    A_sb[:, k * 512:(k + 1) * 512], T1T[:, k * 512:(k + 1) * 512],
                    zeros[:, :], 0.0, ALU.add, ALU.add), after=[tT1[k]], tok=True))
            tbias = None
            for k in range(NQT):
                for c in range(1, NCH):
                    tbias = b.emit("vector", lambda v, k=k, c=c: v.tensor_scalar(
                        bq[:, k * NCH + c: k * NCH + c + 1],
                        rt(A_sb, NQT)[:, k, c * C - 1:c * C],
                        1.0 / 16, LN8, ALU.mult, ALU.add),
                        after=[tA[k], tqmul] if c == 1 else [tA[k]], tok=False)
                    tbias = b.emit("vector", lambda v, k=k, c=c: v.tensor_scalar(
                        bk[:, k * NCH + c: k * NCH + c + 1],
                        rt(A_sb, NQT)[:, k, c * C - 1:c * C],
                        -1.0 / 16, 0.0, ALU.mult, ALU.add),
                        after=(), tok=(k == NQT - 1 and c == NCH - 1))
            texpq = texpk = None
            for k in range(NQT):
                for c in range(NCH):
                    bias_q = per["cbias"][:, 2:3] if c == 0 else bq[:, k * NCH + c: k * NCH + c + 1]
                    bias_k = per["cbias"][:, 3:4] if c == 0 else bk[:, k * NCH + c: k * NCH + c + 1]
                    texpq = b.emit("scalar", lambda s, k=k, c=c, bias_q=bias_q: s.activation(
                        expq[:, k * 512 + c * C: k * 512 + (c + 1) * C],
                        A_sb[:, k * 512 + c * C: k * 512 + (c + 1) * C],
                        AF.Exp, scale=-1.0 / 16, bias=bias_q),
                        after=[tA[k], tbias, tqmul], tok=True)
                    texpk = b.emit("scalar", lambda s, k=k, c=c, bias_k=bias_k: s.activation(
                        expk[:, k * 512 + c * C: k * 512 + (c + 1) * C],
                        A_sb[:, k * 512 + c * C: k * 512 + (c + 1) * C],
                        AF.Exp, scale=1.0 / 16, bias=bias_k),
                        after=[tA[k], tbias, tkmul], tok=True)
            tdprev = None
            for c in range(NCH):
                g = w * NCH + c
                if c == 0:
                    td = b.emit("vector", lambda v: v.tensor_copy(
                        dcol[:, :], rt(A_sb, NQT)[:, :, C - 1]),
                        after=tA + [tdprev], tok=True)
                else:
                    td = b.emit("vector", lambda v, c=c: v.tensor_tensor(
                        dcol[:, :], rt(A_sb, NQT)[:, :, (c + 1) * C - 1],
                        rt(A_sb, NQT)[:, :, c * C - 1], ALU.subtract),
                        after=tA + [tdprev], tok=True)
                texp = b.emit("scalar", lambda s, g=g: s.activation(
                    per["expaC"][:, g * 8:(g + 1) * 8], dcol[:, :], AF.Exp, scale=-1.0 / 16),
                    after=[td], tok=True)
                tdprev = texp
                tlc = b.emit("vector", lambda v, g=g: v.tensor_copy(
                    per["lam"][:, g * 8:(g + 1) * 8], per["lamrun"][:, :]),
                    after=[tlam], tok=True)
                tlam = b.emit("vector", lambda v, g=g: v.tensor_tensor(
                    per["lamrun"][:, :], per["lamrun"][:, :],
                    per["expaC"][:, g * 8:(g + 1) * 8], ALU.mult),
                    after=[texp, tlc], tok=True)

            tqmul = b.emit("vector", lambda v: v.tensor_tensor(
                qt[:, :], Qraw[:, :], expq[:, :], ALU.mult),
                after=[texpq, tq_done] + list(tcpq.values()), tok=True)
            tkmul = b.emit("vector", lambda v: v.tensor_tensor(
                kt_sb[:, :], Kraw[:, :], expk[:, :], ALU.mult),
                after=[texpk, tk_done] + list(tcpk.values()), tok=True)
            tq_done = b.emit("sync", lambda s, w=w: s.dma_start(
                out=io["sc_qtT"][w], in_=qt[:, :]), after=[tqmul], dma="h_p1q")
            tk_done = b.emit("sync", lambda s, w=w: s.dma_start(
                out=io["sc_ktT"][w], in_=kt_sb[:, :]), after=[tkmul], dma="h_p1k")
        b.barrier()


POS = [(h % 2) * 8 + h // 2 for h in range(H)]


def phase2(b, io, per, windows=NW, stage_w=True):
    nc = b.nc
    with ExitStack() as ph:
        pe_ = ph.enter_context
        wv_sb = pe_(nc.sbuf_tensor("p2_wv", [128, NKT * VD], BF16))
        xnT2 = pe_(nc.sbuf_tensor("p2_xnT", [128, 2 * NKT * 512], BF16))
        qt2 = pe_(nc.sbuf_tensor("p2_qt", [128, 2 * NQT * 512], BF16))
        kt2 = pe_(nc.sbuf_tensor("p2_kt", [128, 2 * NQT * 512], BF16))
        ktn = pe_(nc.sbuf_tensor("p2_ktn", [128, NQT * 128], BF16))
        V_sb = pe_(nc.sbuf_tensor("p2_V", [128, VD], BF16))
        PTm = pe_(nc.sbuf_tensor("p2_PTm", [128, 2 * 128], BF16))
        S_sb = pe_(nc.sbuf_tensor("p2_S", [128, NQT * 128], F32))
        S_bf = pe_(nc.sbuf_tensor("p2_Sbf", [128, NQT * 128], BF16))
        Stmp = pe_(nc.sbuf_tensor("p2_Stmp", [128, NQT * 128], F32))
        OTw = pe_(nc.sbuf_tensor("p2_OTw", [128, 16 * 512], BF16))
        wstg = pe_(nc.sbuf_tensor("p2_wstg", [128, 6 * D], BF16))
        pVO = pe_(nc.psum_tensor("p2_pVO", [128, 2048], F32))
        pPT = [pe_(nc.psum_tensor(f"p2_pPT{i}", [128, 512], F32)) for i in range(2)]
        pKU = pe_(nc.psum_tensor("p2_pKU", [128, 1024], F32))
        pK = pKU[:, 0:512].bitcast(BF16)

        tw = b.emit("gpsimd", lambda g: g.dma_start(
            out=wv_sb[:, :].rearrange("p (k n) -> p k n", k=NKT),
            in_=io["wv"].rearrange("(k p) n -> p k n", p=128)), dma="g_p2w")
        b.emit("vector", lambda v: v.memset(S_sb[:, :], 0.0), tok=True)
        tS0 = b.emit("vector", lambda v: v.memset(S_bf[:, :], 0.0), tok=True)

        stage_jobs = []
        if stage_w:
            for e_ in range(E):
                for which in (0, 1):
                    src = (io["w1T"][e_].rearrange("(k p) n -> p k n", p=128) if which == 0
                           else io["w2T"][e_].rearrange("(k p) n -> p k n", p=128))
                    dst = io["sc_w1b"][e_] if which == 0 else io["sc_w2b"][e_]
                    kk = NKT if which == 0 else NHT
                    nn = HH if which == 0 else D
                    h1 = kk // 2
                    stage_jobs.append((src[:, 0:h1, :], dst[:, 0:h1 * nn], h1))
                    stage_jobs.append((src[:, h1:kk, :], dst[:, h1 * nn:kk * nn], kk - h1))
        tstg = None

        def stage_some(njobs, gate_tok):
            nonlocal tstg
            for _ in range(njobs):
                if not stage_jobs:
                    return
                src, dst, kk = stage_jobs.pop(0)
                nel = src.shape[1] * src.shape[2]
                tg_ = b.emit("gpsimd", lambda g, src=src, kk=kk, nel=nel: g.dma_start(
                    out=wstg[:, 0:nel].rearrange("p (k n) -> p k n", k=kk),
                    in_=src), after=[tstg, gate_tok], dma="g_wstg")
                tstg = b.emit("scalar", lambda s, dst=dst, nel=nel: s.dma_start(
                    out=dst, in_=wstg[:, 0:nel]), after=[tg_], dma="h_wstg")

        tVml = {0: None, 1: None}
        tOel = None
        tSul = tS0
        txn = tqt = tkt = {}
        tOdma = None
        tVml_w = {0: None, 1: None}
        tqk_w = {0: None, 1: None}
        txn_t = {}
        tqt_t = {}
        tkt_t = {}
        for w in range(windows):
            wp = w % 2
            xnT = xnT2[:, wp * NKT * 512:(wp + 1) * NKT * 512]
            qt = qt2[:, wp * NQT * 512:(wp + 1) * NQT * 512]
            kt_sb = kt2[:, wp * NQT * 512:(wp + 1) * NQT * 512]
            txn = b.emit("sync", lambda s, w=w, xnT=xnT: s.dma_start(
                out=xnT, in_=io["sc_xnT"][w]), after=[tVml_w[wp]], dma="h_p2xn")
            tqt = b.emit("sync", lambda s, w=w, qt=qt: s.dma_start(
                out=qt, in_=io["sc_qtT"][w]), after=[tqk_w[wp]], dma="h_p2q")
            tkt = b.emit("sync", lambda s, w=w, kt_sb=kt_sb: s.dma_start(
                out=kt_sb, in_=io["sc_ktT"][w]), after=[tqk_w[wp]], dma="h_p2k")
            for c in range(NCH):
                g = w * NCH + c
                tV = None
                for nb in range(4):
                    for k in range(NKT):
                        tV = b.emit("tensor", lambda t, k=k, nb=nb, c=c, xnT=xnT: t.matmul(
                            pVO[:, nb * 512:(nb + 1) * 512],
                            xnT[:, k * 512 + c * 128: k * 512 + (c + 1) * 128],
                            wv_sb[:, k * VD + nb * 512: k * VD + (nb + 1) * 512],
                            start=(k == 0), stop=(k == NKT - 1)),
                            after=[txn, tw, tOel] if (k == 0 and nb == 0) else (),
                            tok=(k == NKT - 1))
                tVe = b.emit("vector", lambda v: v.tensor_copy(V_sb[:, :], pVO[:, :]),
                             after=[tV], tok=True)
                tkn = None
                for mb in range(NQT):
                    tkn = b.emit("tensor", lambda t, mb=mb, c=c, kt_sb=kt_sb: t.transpose(
                        pK[:, mb * 128:(mb + 1) * 128],
                        kt_sb[:, mb * 512 + c * 128: mb * 512 + (c + 1) * 128],
                        per["ident"][:, :]),
                        after=[tkt, tSul] if mb == 0 else (), tok=(mb == NQT - 1))
                tkne = b.emit("vector", lambda v: v.tensor_copy(ktn[:, :], pK[:, :]),
                              after=[tkn], tok=True)
                tPTm = {0: None, 1: None}
                tOx = {}
                tOi = {}
                for h in range(H):
                    mb, off, par, pos = h // 2, 64 * (h % 2), h % 2, POS[h]
                    tPT = b.emit("tensor", lambda t, mb=mb, off=off, par=par, c=c, kt_sb=kt_sb, qt=qt: t.matmul(
                        pPT[par][:, 0:128],
                        kt_sb[off:off + 64, mb * 512 + c * 128: mb * 512 + (c + 1) * 128],
                        qt[off:off + 64, mb * 512 + c * 128: mb * 512 + (c + 1) * 128],
                        start=True, stop=True),
                        after=[tqt, tkt, tPTm[par], tOx.get(h - 2)], tok=True)
                    tPTm[par] = b.emit("vector", lambda v, par=par: v.tensor_tensor(
                        PTm[:, par * 128:(par + 1) * 128], pPT[par][:, 0:128],
                        per["cmask"][:, :], ALU.mult), after=[tPT], tok=True)
                    tOi[h] = b.emit("tensor", lambda t, mb=mb, off=off, pos=pos, c=c, qt=qt: t.matmul(
                        pVO[:, pos * 128:(pos + 1) * 128],
                        S_bf[off:off + 64, mb * 128:(mb + 1) * 128],
                        qt[off:off + 64, mb * 512 + c * 128: mb * 512 + (c + 1) * 128],
                        start=True, stop=False),
                        after=[tSul, tVe, tqt], tok=False)
                    tOx[h] = b.emit("tensor", lambda t, h=h, par=par, pos=pos: t.matmul(
                        pVO[:, pos * 128:(pos + 1) * 128],
                        V_sb[:, h * 128:(h + 1) * 128],
                        PTm[:, par * 128:(par + 1) * 128],
                        start=False, stop=True),
                        after=[tPTm[par]], tok=True)
                tU = None
                for h in range(H):
                    mb, off = h // 2, 64 * (h % 2)
                    tU = b.emit("tensor", lambda t, h=h, mb=mb, off=off: t.matmul(
                        pKU[off:off + 64, mb * 128:(mb + 1) * 128],
                        ktn[:, mb * 128 + off: mb * 128 + off + 64],
                        V_sb[:, h * 128:(h + 1) * 128],
                        start=True, stop=True,
                        tile_position=(0, off)),
                        after=[tkne, tVe] if h == 0 else (), tok=(h == H - 1))
                tSu1 = b.emit("vector", lambda v: v.tensor_tensor(
                    Stmp[:, :], S_sb[:, :], pKU[:, :], ALU.add),
                    after=[tU] + list(tOi.values()), tok=True)
                tSu = None
                for kti in range(NQT):
                    tSu = b.emit("vector", lambda v, kti=kti, g=g: v.tensor_scalar_mul(
                        S_sb[:, kti * 128:(kti + 1) * 128],
                        Stmp[:, kti * 128:(kti + 1) * 128],
                        per["expaC"][:, g * 8 + kti: g * 8 + kti + 1]),
                        after=[tSu1], tok=(kti == NQT - 1))
                tSul = b.emit("vector", lambda v: v.tensor_copy(S_bf[:, :], S_sb[:, :]),
                              after=[tSu] + list(tOi.values()), tok=True)
                tOel = b.emit("vector", lambda v, c=c: v.tensor_copy(
                    OTw[:, :].rearrange("p (q t) -> p q t", q=16)[:, :, c * 128:(c + 1) * 128],
                    pVO[:, :].rearrange("p (q t) -> p q t", q=16)),
                    after=[tOdma] + list(tOx.values()) + list(tOi.values()), tok=True)
                tVml_cur = tV
            tOdma = b.emit("sync", lambda s, w=w: s.dma_start(
                out=io["sc_OT"][w], in_=OTw[:, :]), after=[tOel], dma="h_p2o")
            tVml_w[wp] = tVml_cur
            tqk_w[wp] = tOel
            stage_some(4, tOdma)
        stage_some(99, None)
        b.emit("sync", lambda s: s.dma_start(
            out=io["cc_in"].ap(), in_=S_sb[:, :]), after=[tSul], dma="h_p2s")
        b.barrier()


def phase_cc(b, io, per):
    nc = b.nc
    tcc = b.emit("gpsimd", lambda g: g.collective_compute(
        "AllGather", ALU.bypass,
        replica_groups=[[0, 1], [2, 3], [4, 5], [6, 7]],
        ins=[io["cc_in"].ap()], outs=[io["cc_out"].ap()]), tok=True)
    t1 = b.emit("sync", lambda s: s.dma_start(
        out=per["Sin"][:, :], in_=io["cc_out"].ap()[0:128, :]), after=[tcc], dma="h_cc")
    t2 = b.emit("sync", lambda s: s.dma_start(
        out=per["smask"][:, :], in_=io["smask"]), dma="h_cc")
    t3 = b.emit("vector", lambda v: v.tensor_scalar_mul(
        per["Sin"][:, :], per["Sin"][:, :], per["smask"][:, 0:1]),
        after=[t1, t2], tok=True)
    b.emit("vector", lambda v: v.tensor_copy(per["Sinb"][:, :], per["Sin"][:, :]),
           after=[t3], tok=True)
    b.barrier()


def phase3a(b, io, per, windows=NW):
    nc = b.nc
    with ExitStack() as ph:
        pe_ = ph.enter_context
        wg_sb = pe_(nc.sbuf_tensor("p3_wg", [128, NKT * VD], BF16))
        xnT = pe_(nc.sbuf_tensor("p3_xnT", [128, NKT * 512], BF16))
        OTr = pe_(nc.sbuf_tensor("p3_OTr", [128, 16 * 512], BF16))
        qt = pe_(nc.sbuf_tensor("p3_qt", [128, NQT * 512], BF16))
        qtl = pe_(nc.sbuf_tensor("p3_qtl", [128, NQT * 128], BF16))
        silG = pe_(nc.sbuf_tensor("p3_silG", [128, 16 * 512], BF16))
        OTf = pe_(nc.sbuf_tensor("p3_OTf", [128, 16 * 512], BF16))
        OT2c = pe_(nc.sbuf_tensor("p3_OT2c", [128, 16 * 128], BF16))
        o1c = pe_(nc.sbuf_tensor("p3_o1c", [128, 16 * 128], BF16))
        og = pe_(nc.sbuf_tensor("p3_og", [128, 16 * 512], BF16))
        rln = pe_(nc.sbuf_tensor("p3_rln", [128, 16], F32))
        rstd = pe_(nc.sbuf_tensor("p3_rstd", [128, 16], F32))
        rstdT = pe_(nc.sbuf_tensor("p3_rstdT", [16, 128], F32))
        pG = [pe_(nc.psum_tensor(f"p3_pG{i}", [128, 512], F32)) for i in range(2)]
        pCB = pe_(nc.psum_tensor("p3_pCB", [128, 2048], F32))

        tw = b.emit("gpsimd", lambda g: g.dma_start(
            out=wg_sb[:, :].rearrange("p (k n) -> p k n", k=NKT),
            in_=io["wg"].rearrange("(k p) n -> p k n", p=128)), dma="g_p3w")

        tGl = tcorl = togl = togdma = None
        tpg0 = tpg1 = tsilfree = None
        for w in range(windows):
            txn = b.emit("sync", lambda s, w=w: s.dma_start(
                out=xnT[:, :], in_=io["sc_xnT"][w]), after=[tGl], dma="h_p3xn")
            tot = b.emit("sync", lambda s, w=w: s.dma_start(
                out=OTr[:, :], in_=io["sc_OT"][w]), after=[tcorl], dma="h_p3o")
            tqt = b.emit("sync", lambda s, w=w: s.dma_start(
                out=qt[:, :], in_=io["sc_qtT"][w]), after=[tcorl], dma="h_p3q")
            tse = {}
            for h in range(H):
                pp = pG[h % 2]
                tG = None
                for k in range(NKT):
                    tG = b.emit("tensor", lambda t, h=h, k=k, pp=pp: t.matmul(
                        pp[:, :], wg_sb[:, k * VD + h * 128: k * VD + (h + 1) * 128],
                        xnT[:, k * 512:(k + 1) * 512],
                        start=(k == 0), stop=(k == NKT - 1)),
                        after=[txn, tw, tse.get(h - 2), tpg0 if h == 0 else None,
                               tpg1 if h == 1 else None] if k == 0 else (),
                        tok=(k == NKT - 1))
                tse[h] = b.emit("scalar", lambda s, h=h, pp=pp: s.activation(
                    silG[:, POS[h] * 512:(POS[h] + 1) * 512], pp[:, :], AF.Silu),
                    after=[tG, tsilfree], tok=True)
            tGl = tG
            tsil = tse[H - 1]
            tstats = None
            tOTf = None
            for c in range(NCH):
                g = w * NCH + c
                tql = None
                for mb in range(NQT):
                    tql = b.emit("vector", lambda v, mb=mb, c=c, g=g: v.tensor_scalar_mul(
                        qtl[:, mb * 128:(mb + 1) * 128],
                        qt[:, mb * 512 + c * 128: mb * 512 + (c + 1) * 128],
                        per["lam"][:, g * 8 + mb: g * 8 + mb + 1]),
                        after=[tqt, tcorl], tok=(mb == NQT - 1))
                tcor = {}
                for h in range(H):
                    mb, off, pos = h // 2, 64 * (h % 2), POS[h]
                    tcor[h] = b.emit("tensor", lambda t, mb=mb, off=off, pos=pos: t.matmul(
                        pCB[:, pos * 128:(pos + 1) * 128],
                        per["Sinb"][off:off + 64, mb * 128:(mb + 1) * 128],
                        qtl[off:off + 64, mb * 128:(mb + 1) * 128],
                        start=True, stop=True),
                        after=[tql, togl] if h == 0 else (), tok=(h == H - 1))
                tOTf = b.emit("vector", lambda v, c=c: v.tensor_tensor(
                    rt(OTf, 16)[:, :, c * 128:(c + 1) * 128],
                    rt(OTr, 16)[:, :, c * 128:(c + 1) * 128],
                    pCB[:, :].rearrange("p (q t) -> p q t", q=16), ALU.add),
                    after=[tcor[H - 1], tot, togl], tok=True)
                tcorl = tOTf
                t2c = b.emit("scalar", lambda s, c=c: s.activation(
                    OT2c[:, :].rearrange("p (q t) -> p q t", q=16),
                    rt(OTf, 16)[:, :, c * 128:(c + 1) * 128], AF.Square),
                    after=[tOTf, tstats], tok=True)
                tst = {}
                for h in range(H):
                    pos = POS[h]
                    tst[h] = b.emit("tensor", lambda t, pos=pos, h=h: t.matmul(
                        pG[0][0:128, h:h + 1], OT2c[:, pos * 128:(pos + 1) * 128],
                        per["ones_col"][:, :], start=True, stop=True),
                        after=[t2c, tse.get(H - 1), tstats] if h == 0 else (),
                        tok=(h == H - 1))
                trl = b.emit("scalar", lambda s: s.activation(
                    rln[:, :], pG[0][0:128, 0:16], AF.Ln, scale=1.0 / DV,
                    bias=per["cbias"][:, 0:1]), after=[tst[H - 1]], tok=True)
                trs = b.emit("scalar", lambda s: s.activation(
                    rstd[:, :], rln[:, :], AF.Exp, scale=-0.5), after=[trl], tok=True)
                ttr = b.emit("tensor", lambda t: t.transpose(
                    pG[1][0:16, 0:128], rstd[:, :], per["identf"][:, :]),
                    after=[trs], tok=True)
                tte = b.emit("vector", lambda v: v.tensor_copy(
                    rstdT[:, :], pG[1][0:16, 0:128]), after=[ttr], tok=True)
                tB = {}
                for h in range(H):
                    pos = POS[h]
                    tB[h] = b.emit("tensor", lambda t, h=h, pos=pos: t.matmul(
                        pCB[:, pos * 128:(pos + 1) * 128],
                        per["sel16"][0:16, h * 128:(h + 1) * 128],
                        rstdT[0:16, :], start=True, stop=True),
                        after=[tte, tOTf] if h == 0 else (), tok=(h == H - 1))
                to1 = b.emit("vector", lambda v, c=c: v.scalar_tensor_tensor(
                    o1c[:, :].rearrange("p (q t) -> p q t", q=16),
                    rt(OTf, 16)[:, :, c * 128:(c + 1) * 128],
                    per["normw"][:, 0:1],
                    rt(silG, 16)[:, :, c * 128:(c + 1) * 128],
                    ALU.mult, ALU.mult), after=[tOTf, tsil], tok=True)
                togl = b.emit("vector", lambda v, c=c: v.tensor_tensor(
                    rt(og, 16)[:, :, c * 128:(c + 1) * 128],
                    o1c[:, :].rearrange("p (q t) -> p q t", q=16),
                    pCB[:, :].rearrange("p (q t) -> p q t", q=16), ALU.mult),
                    after=[to1, tB[H - 1], togdma], tok=True)
                tstats = togl
                tpg0, tpg1 = trl, tte
                tsilfree = to1
            togdma = b.emit("sync", lambda s, w=w: s.dma_start(
                out=io["sc_og"][w], in_=og[:, :]), after=[togl], dma="h_p3og")
        b.barrier()


def phase3b(b, io, per, windows=NW):
    nc = b.nc
    with ExitStack() as ph:
        pe_ = ph.enter_context
        wo_sb = pe_(nc.sbuf_tensor("p3b_wo", [128, NKT * D], BF16))
        gw_sb = pe_(nc.sbuf_tensor("p3b_gw", [128, NKT * E], BF16))
        gb_sb = pe_(nc.sbuf_tensor("p3b_gb", [E, 1], F32))
        gbias_sb = pe_(nc.sbuf_tensor("p3b_gbias", [E, 1], F32))
        og = pe_(nc.sbuf_tensor("p3b_og", [128, 16 * 512], BF16))
        xT = pe_(nc.sbuf_tensor("p3b_xT", [128, NKT * 512], F32))
        r1T = pe_(nc.sbuf_tensor("p3b_r1", [128, NKT * 512], F32))
        sq2 = og
        xn2T = pe_(nc.sbuf_tensor("p3b_xn2", [128, NKT * 512], BF16))
        rows = pe_(nc.sbuf_tensor("p3b_rows", [1, 4 * 512], F32))
        tmpS = pe_(nc.sbuf_tensor("p3b_tmpS", [128, 2 * 512], BF16))
        scs = pe_(nc.sbuf_tensor("p3b_scs", [8, 512], F32))
        scn = pe_(nc.sbuf_tensor("p3b_scn", [128, 8], F32))
        tcols = pe_(nc.sbuf_tensor("p3b_tcols", [128, 8], F32))
        comb = pe_(nc.sbuf_tensor("p3b_comb", [128, 8], F32))
        P = [pe_(nc.psum_tensor(f"p3b_P{i}", [128, 512], F32)) for i in range(6)]

        tw = b.emit("gpsimd", lambda g: g.dma_start(
            out=wo_sb[:, :].rearrange("p (k n) -> p k n", k=NKT),
            in_=io["wo"].rearrange("(k p) n -> p k n", p=128)), dma="g_p3bw")
        tw2 = b.emit("gpsimd", lambda g: g.dma_start(
            out=gw_sb[:, :].rearrange("p (k n) -> p k n", k=NKT),
            in_=io["gwT"].rearrange("(k p) n -> p k n", p=128)), dma="g_p3bw")
        tw3 = b.emit("gpsimd", lambda g: g.dma_start(
            out=gb_sb[:, :], in_=io["gb"].rearrange("o e -> e o")), dma="g_p3bw")
        tw4 = b.emit("gpsimd", lambda g: g.dma_start(
            out=gbias_sb[:, :], in_=io["gbias"].rearrange("o e -> e o")), dma="g_p3bw")
        tw = tw2 = tw3 = tw4

        tr1l = txn2l = tscl = tcombl = togr = tmm2l = tdr1l = txn2d = None
        for w in range(windows):
            tog = b.emit("sync", lambda s, w=w: s.dma_start(
                out=og[:, :], in_=io["sc_og"][w]), after=[togr, tmm2l], dma="h_p3bog")
            tx = b.emit("sync", lambda s, w=w: s.dma_start(
                out=rt(xT[:, :], NKT),
                in_=io["xT"][:, w * 512:(w + 1) * 512].rearrange("(k p) t -> p k t", p=128)),
                after=[tr1l], dma="h_p3bx")
            tr1 = {}
            trr = None
            for dblk in range(NKT):
                pp = P[dblk % 2]
                tO = None
                for vdt in range(NKT):
                    tO = b.emit("tensor", lambda t, dblk=dblk, vdt=vdt, pp=pp: t.matmul(
                        pp[:, :], wo_sb[:, vdt * D + dblk * 128: vdt * D + (dblk + 1) * 128],
                        og[:, POS[vdt] * 512:(POS[vdt] + 1) * 512],
                        start=(vdt == 0), stop=(vdt == NKT - 1)),
                        after=[tog, tw, tr1.get(dblk - 2)] if vdt == 0 else (),
                        tok=(vdt == NKT - 1))
                tr1[dblk] = b.emit("vector", lambda v, dblk=dblk, pp=pp: v.tensor_tensor(
                    r1T[:, dblk * 512:(dblk + 1) * 512], pp[:, :],
                    xT[:, dblk * 512:(dblk + 1) * 512], ALU.add),
                    after=[tO, tx, txn2l, tdr1l], tok=True)
            togr = tO
            tr1l = tr1[NKT - 1]
            tdr1l = b.emit("sync", lambda s, w=w: s.dma_start(
                out=io["sc_r1T"][w], in_=r1T[:, :]), after=[tr1l], dma="h_p3br1")
            tsq = b.emit("scalar", lambda s: s.activation(sq2[:, :], r1T[:, :], AF.Square),
                         after=[tr1l, tscl, togr], tok=True)
            mmt = mmt2 = None
            for k in range(NKT):
                mmt = b.emit("tensor", lambda t, k=k: t.matmul(
                    P[2][0:1, :], per["ones_colf"][:, :], r1T[:, k * 512:(k + 1) * 512],
                    start=(k == 0), stop=(k == NKT - 1)),
                    after=[tr1[k], tcombl] if k == 0 else [tr1[k]], tok=(k == NKT - 1))
            for k in range(NKT):
                mmt2 = b.emit("tensor", lambda t, k=k: t.matmul(
                    P[3][0:1, :], per["ones_col"][:, :],
                    sq2[:, k * 512:(k + 1) * 512],
                    start=(k == 0), stop=(k == NKT - 1)),
                    after=[tsq] if k == 0 else (), tok=(k == NKT - 1))
            t1 = b.emit("vector", lambda v: v.tensor_scalar_mul(
                rows[0:1, 0:512], P[2][0:1, :], 1.0 / D), after=[mmt], tok=True)
            tmm2l = mmt2
            t2 = b.emit("vector", lambda v: v.tensor_scalar_mul(
                rows[0:1, 512:1024], P[3][0:1, :], 1.0 / D), after=[mmt2], tok=True)
            t3 = b.emit("vector", lambda v: v.tensor_tensor(
                rows[0:1, 1024:1536], rows[0:1, 0:512], rows[0:1, 0:512], ALU.mult),
                after=[t1], tok=True)
            t4 = b.emit("vector", lambda v: v.tensor_tensor(
                rows[0:1, 1536:2048], rows[0:1, 512:1024], rows[0:1, 1024:1536], ALU.subtract),
                after=[t2, t3], tok=True)
            t5 = b.emit("scalar", lambda s: s.activation(
                rows[0:1, 512:1024], rows[0:1, 1536:2048], AF.Ln, bias=per["cbias"][0:1, 0:1]),
                after=[t4], tok=True)
            t6 = b.emit("scalar", lambda s: s.activation(
                rows[0:1, 1024:1536], rows[0:1, 512:1024], AF.Exp, scale=-0.5),
                after=[t5, t3], tok=True)
            t7 = b.emit("vector", lambda v: v.tensor_tensor(
                rows[0:1, 1536:2048], rows[0:1, 0:512], rows[0:1, 1024:1536], ALU.mult),
                after=[t6, t4], tok=True)
            tb1 = b.emit("tensor", lambda t: t.matmul(
                P[4][:, :], per["ones1f"][:, :], rows[0:1, 1024:1536], start=True, stop=True),
                after=[t6, txn2l], tok=True)
            tb2 = b.emit("tensor", lambda t: t.matmul(
                P[5][:, :], per["ones1f"][:, :], rows[0:1, 1536:2048], start=True, stop=True),
                after=[t7], tok=True)
            txn2 = None
            tprev = {0: None, 1: None}
            for k in range(NKT):
                par = k % 2
                ta = b.emit("vector", lambda v, k=k, par=par: v.tensor_tensor(
                    tmpS[:, par * 512:(par + 1) * 512], r1T[:, k * 512:(k + 1) * 512],
                    P[4][:, :], ALU.mult),
                    after=[tb1, mmt, mmt2, tprev[par]], tok=True)
                txn2 = b.emit("vector", lambda v, k=k, par=par: v.tensor_tensor(
                    xn2T[:, k * 512:(k + 1) * 512], tmpS[:, par * 512:(par + 1) * 512],
                    P[5][:, :], ALU.subtract), after=[tb2, ta, txn2d], tok=True)
                tprev[par] = txn2
            txn2l = txn2
            txn2d = b.emit("sync", lambda s, w=w: s.dma_start(
                out=io["sc_xn2T"][w], in_=xn2T[:, :]), after=[txn2], dma="h_p3bxn2")
            tsc = None
            for k in range(NKT):
                tsc = b.emit("tensor", lambda t, k=k: t.matmul(
                    P[2][0:8, :], gw_sb[:, k * E:(k + 1) * E],
                    xn2T[:, k * 512:(k + 1) * 512],
                    start=(k == 0), stop=(k == NKT - 1)),
                    after=[txn2, tw2, t1] if k == 0 else (), tok=(k == NKT - 1))
            tsce = b.emit("scalar", lambda s: s.copy(scs[:, :], P[2][0:8, :]),
                          after=[tsc], tok=True)
            tscl = b.emit("vector", lambda v: v.tensor_scalar(
                scs[:, :], scs[:, :], gb_sb[0:8, 0:1], gbias_sb[0:8, 0:1],
                ALU.add, ALU.add), after=[tsce, tw3, tw4], tok=True)
            tcomb = None
            for c in range(NCH):
                ttr = b.emit("tensor", lambda t, c=c: t.transpose(
                    P[3][0:128, 0:8],
                    scs[:, c * 128:(c + 1) * 128], per["identf"][0:8, 0:8]),
                    after=[tscl, tcomb], tok=True)
                tte = b.emit("vector", lambda v: v.tensor_copy(
                    scn[:, :], P[3][0:128, 0:8]), after=[ttr], tok=True)
                tm1 = b.emit("vector", lambda v: v.tensor_reduce(
                    tcols[:, 0:1], scn[:, :], mybir.AxisListType.X, ALU.max),
                    after=[tte], tok=True)
                te1 = b.emit("vector", lambda v: v.tensor_single_scalar(
                    comb[:, 0:8], scn[:, :], tcols[:, 0:1], ALU.is_equal),
                    after=[tm1], tok=True)
                tmsk = b.emit("vector", lambda v: v.scalar_tensor_tensor(
                    scn[:, :], comb[:, 0:8], -1e30, scn[:, :], ALU.mult, ALU.add),
                    after=[te1], tok=True)
                tm2 = b.emit("vector", lambda v: v.tensor_reduce(
                    tcols[:, 1:2], scn[:, :], mybir.AxisListType.X, ALU.max),
                    after=[tmsk], tok=True)
                te2 = b.emit("vector", lambda v: v.tensor_single_scalar(
                    scn[:, :], scn[:, :], tcols[:, 1:2], ALU.is_equal),
                    after=[tm2], tok=True)
                td = b.emit("vector", lambda v: v.tensor_tensor(
                    tcols[:, 2:3], tcols[:, 1:2], tcols[:, 0:1], ALU.subtract),
                    after=[te2], tok=True)
                te21 = b.emit("scalar", lambda s: s.activation(
                    tcols[:, 3:4], tcols[:, 2:3], AF.Exp), after=[td], tok=True)
                tden = b.emit("vector", lambda v: v.tensor_scalar_add(
                    tcols[:, 4:5], tcols[:, 3:4], 1.0), after=[te21], tok=True)
                tg1 = b.emit("vector", lambda v: v.reciprocal(
                    tcols[:, 5:6], tcols[:, 4:5]), after=[tden], tok=True)
                tg2 = b.emit("vector", lambda v: v.tensor_scalar(
                    tcols[:, 6:7], tcols[:, 5:6], -1.0, 1.0, ALU.mult, ALU.add),
                    after=[tg1], tok=True)
                tc1 = b.emit("vector", lambda v: v.tensor_scalar_mul(
                    comb[:, 0:8], comb[:, 0:8], tcols[:, 5:6]), after=[tg1], tok=True)
                tc2 = b.emit("vector", lambda v: v.scalar_tensor_tensor(
                    comb[:, 0:8], scn[:, :], tcols[:, 6:7], comb[:, 0:8],
                    ALU.mult, ALU.add), after=[tg2, tc1], tok=True)
                ttc = b.emit("tensor", lambda t, c=c: t.transpose(
                    P[3][0:8, 128:256], comb[:, 0:8], per["identf"][:, :]),
                    after=[tc2, tte], tok=True)
                tcomb = b.emit("scalar", lambda s, w=w, c=c: s.copy(
                    per["combT"][0:8, w * 512 + c * 128: w * 512 + (c + 1) * 128],
                    P[3][0:8, 128:256]), after=[ttc], tok=True)
            tcombl = tcomb
        b.barrier()


def phase_ffn(b, io, per, windows=NW):
    nc = b.nc
    with ExitStack() as ph:
        pe_ = ph.enter_context
        w1b = pe_(nc.sbuf_tensor("f_w1b", [128, 2 * NKT * HH], BF16))
        w2s = pe_(nc.sbuf_tensor("f_w2s", [128, 2 * NHT * 512], BF16))
        ht = pe_(nc.sbuf_tensor("f_ht", [128, 2 * NHT * 512], BF16))
        xn2T = pe_(nc.sbuf_tensor("f_xn2", [128, NKT * 512], BF16))
        yT = pe_(nc.sbuf_tensor("f_y", [128, NKT * 512], F32))
        cb = pe_(nc.sbuf_tensor("f_cb", [128, 2 * 512], BF16))
        htmp = pe_(nc.sbuf_tensor("f_htmp", [128, 2 * 512], BF16))
        b1r_sb = pe_(nc.sbuf_tensor("f_b1r", [128, E * NHT], F32))
        b2_sb = pe_(nc.sbuf_tensor("f_b2", [E, D], BF16))
        pH = [pe_(nc.psum_tensor(f"f_pH{i}", [128, 512], F32)) for i in range(2)]
        pY = [pe_(nc.psum_tensor(f"f_pY{i}", [128, 512], F32)) for i in range(4)]
        pC = pe_(nc.psum_tensor("f_pC", [128, 512], F32))

        tb1 = b.emit("sync", lambda s: s.dma_start(
            out=b1r_sb[:, :], in_=io["b1r"]), dma="h_fb")
        tb2l = b.emit("gpsimd", lambda g: g.dma_start(
            out=b2_sb[:, :], in_=io["b2"]), dma="g_fb")

        txl = None
        tsil = {}
        typrev = {}
        tydl = None
        tHlast = {0: None, 1: None}
        tcbl = {0: None, 1: None}
        for w in range(windows):
            tx = b.emit("sync", lambda s, w=w: s.dma_start(
                out=xn2T[:, :], in_=io["sc_xn2T"][w]), after=[txl], dma="h_fx")
            tyi = b.emit("sync", lambda s, w=w: s.dma_start(
                out=yT[:, :], in_=io["sc_r1T"][w]), after=[tydl], dma="h_fy")
            for rnd in range(4):
                tw1 = {}
                for je in range(2):
                    e_ = rnd * 2 + je
                    par = e_ % 2
                    tw1[je] = b.emit("gpsimd", lambda g, e_=e_, par=par: g.dma_start(
                        out=w1b[:, par * NKT * HH:(par + 1) * NKT * HH],
                        in_=io["sc_w1b"][e_]),
                        after=[tHlast[par]], dma="g_fw1")
                for je in range(2):
                    e_ = rnd * 2 + je
                    par = e_ % 2
                    tcb = b.emit("tensor", lambda t, e_=e_, w=w: t.matmul(
                        pC[:, 0:512], per["sel8"][0:8, e_ * 128:(e_ + 1) * 128],
                        per["combT"][0:8, w * 512:(w + 1) * 512],
                        start=True, stop=True), after=[tcbl[je], tx], tok=True)
                    tcbl[je] = b.emit("scalar", lambda s, je=je: s.copy(
                        cb[:, je * 512:(je + 1) * 512], pC[:, 0:512]),
                        after=[tcb, tsil.get((1 - je, NHT - 1)), tsil.get((je, NHT - 1))],
                        tok=True)
                    for mb in range(NHT):
                        pp = pH[mb % 2]
                        tH = None
                        xdep = tsil.get((1 - je, NHT - 1 if mb % 2 == (NHT - 1) % 2
                                          else NHT - 2)) if mb < 2 else None
                        for k in range(NKT):
                            tH = b.emit("tensor", lambda t, mb=mb, k=k, pp=pp, par=par: t.matmul(
                                pp[:, :],
                                w1b[:, par * NKT * HH + k * HH + mb * 128:
                                    par * NKT * HH + k * HH + (mb + 1) * 128],
                                xn2T[:, k * 512:(k + 1) * 512],
                                start=(k == 0), stop=(k == NKT - 1)),
                                after=[tw1[je], tx, tsil.get((je, mb - 2)), typrev.get(3), xdep]
                                if k == 0 else (), tok=(k == NKT - 1))
                        par2 = mb % 2
                        tse = b.emit("scalar", lambda s, mb=mb, e_=e_, pp=pp, par2=par2: s.activation(
                            htmp[:, par2 * 512:(par2 + 1) * 512],
                            pp[:, :], AF.Silu,
                            bias=b1r_sb[:, e_ * NHT + mb: e_ * NHT + mb + 1]),
                            after=[tH, tb1, tsil.get((je, mb - 2)), xdep], tok=True)
                        tsil[(je, mb)] = b.emit("vector", lambda v, mb=mb, je=je, par2=par2: v.tensor_tensor(
                            ht[:, (je * NHT + mb) * 512:(je * NHT + mb + 1) * 512],
                            htmp[:, par2 * 512:(par2 + 1) * 512],
                            cb[:, je * 512:(je + 1) * 512], ALU.mult),
                            after=[tse, tcbl[je]], tok=True)
                    tHlast[par] = tH
                thdone = tsil[(1, NHT - 1)]
                thdone0 = tsil[(0, NHT - 1)]
                tw2tok = {}
                tmm = {-1: None}
                for dgrp in range(4):
                    for je in range(2):
                        e_ = rnd * 2 + je
                        gi = dgrp * 2 + je
                        par = gi % 2
                        tw2tok[gi] = b.emit("gpsimd", lambda g, e_=e_, dgrp=dgrp, par=par: g.dma_start(
                            out=w2s[:, par * NHT * 512:(par + 1) * NHT * 512]
                                .rearrange("p (k n) -> p k n", k=NHT),
                            in_=io["sc_w2b"][e_].rearrange("p (k n) -> p k n", k=NHT)
                                [:, :, dgrp * 512:(dgrp + 1) * 512]),
                            after=[tmm.get(gi - 2), tb2l], dma="g_fw2")
                    for je in range(2):
                        e_ = rnd * 2 + je
                        gi = dgrp * 2 + je
                        par = gi % 2
                        for dblk in range(4):
                            d_ = dgrp * 4 + dblk
                            if je == 0 and rnd == 0:
                                b.emit("tensor", lambda t, d_=d_, dblk=dblk, w=w: t.matmul(
                                    pY[dblk][:, :],
                                    b2_sb[0:8, d_ * 128:(d_ + 1) * 128],
                                    per["combT"][0:8, w * 512:(w + 1) * 512],
                                    start=True, stop=False),
                                    after=[thdone, thdone0, typrev.get(dblk)])
                            for kth in range(NHT):
                                last = (je == 1 and kth == NHT - 1)
                                jelast = (dblk == 3 and kth == NHT - 1)
                                first = (je == 0 and kth == 0 and rnd > 0)
                                tmm[gi] = b.emit("tensor", lambda t, je=je, kth=kth, dblk=dblk, par=par, first=first: t.matmul(
                                    pY[dblk][:, :],
                                    w2s[:, par * NHT * 512 + kth * 512 + dblk * 128:
                                        par * NHT * 512 + kth * 512 + (dblk + 1) * 128],
                                    ht[:, (je * NHT + kth) * 512:(je * NHT + kth + 1) * 512],
                                    start=first, stop=last),
                                    after=[tw2tok[gi], thdone, thdone0, typrev.get(dblk)]
                                    if kth == 0 and dblk == 0 else (),
                                    tok=jelast)
                    for dblk in range(4):
                        d_ = dgrp * 4 + dblk
                        tya = b.emit("vector", lambda v, d_=d_, dblk=dblk: v.tensor_tensor(
                            yT[:, d_ * 512:(d_ + 1) * 512], pY[dblk][:, :],
                            yT[:, d_ * 512:(d_ + 1) * 512], ALU.add),
                            after=[tmm[dgrp * 2 + 1], tyi], tok=True)
                        typrev[dblk] = tya
            tydl = b.emit("sync", lambda s, w=w: s.dma_start(
                out=io["outT"][:, w * 512:(w + 1) * 512].rearrange("(k p) t -> p k t", p=128),
                in_=rt(yT[:, :], NKT)), after=[typrev[0], typrev[1], typrev[2], typrev[3]],
                dma="h_fo")
            txl = tx
        if os.environ.get("T_FDBG"):
            dht = nc.dram_tensor("dbg_ht", [128, 2 * NHT * 512], BF16, kind="ExternalOutput").ap()
            dcb = nc.dram_tensor("dbg_cb", [128, 1024], BF16, kind="ExternalOutput").ap()
            toks = b.alltoks()
            b.waitonly("sync", toks)
            b.emit("sync", lambda s: s.dma_start(out=dht, in_=ht[:, :]), dma="h_fdbg")
            b.emit("sync", lambda s: s.dma_start(out=dcb, in_=cb[:, :]), dma="h_fdbg")
        b.barrier()


def build_nc(phases=("p1",), dbg=(), p1_windows=NW,
             p1_stages=("ln", "gate", "gate2", "gate3", "gate4", "proj")):
    nc = bass.Bass("TRN2", target_bir_lowering=False)
    io = {}
    io["xT"] = nc.dram_tensor("xT", [D, TC], F32, kind="ExternalInput").ap()
    io["wq"] = nc.dram_tensor("wq", [D, KD], F32, kind="ExternalInput").ap()
    io["wk"] = nc.dram_tensor("wk", [D, KD], F32, kind="ExternalInput").ap()
    io["wv"] = nc.dram_tensor("wv", [D, VD], F32, kind="ExternalInput").ap()
    io["wg"] = nc.dram_tensor("wg", [D, VD], F32, kind="ExternalInput").ap()
    io["wo"] = nc.dram_tensor("wo", [VD, D], F32, kind="ExternalInput").ap()
    io["wgk1"] = nc.dram_tensor("wgk1", [D, LR], F32, kind="ExternalInput").ap()
    io["wgk2"] = nc.dram_tensor("wgk2", [LR, KD], F32, kind="ExternalInput").ap()
    io["bgk2"] = nc.dram_tensor("bgk2", [1, KD], F32, kind="ExternalInput").ap()
    io["normw"] = nc.dram_tensor("normw", [DV, 1], F32, kind="ExternalInput").ap()
    io["gwT"] = nc.dram_tensor("gwT", [D, E], F32, kind="ExternalInput").ap()
    io["gb"] = nc.dram_tensor("gb", [1, E], F32, kind="ExternalInput").ap()
    io["gbias"] = nc.dram_tensor("gbias", [1, E], F32, kind="ExternalInput").ap()
    io["w1T"] = nc.dram_tensor("w1T", [E, D, HH], F32, kind="ExternalInput").ap()
    io["b1r"] = nc.dram_tensor("b1r", [128, E * NHT], F32, kind="ExternalInput").ap()
    io["w2T"] = nc.dram_tensor("w2T", [E, HH, D], F32, kind="ExternalInput").ap()
    io["b2"] = nc.dram_tensor("b2", [E, D], F32, kind="ExternalInput").ap()
    io["smask"] = nc.dram_tensor("smask", [128, 1], F32, kind="ExternalInput").ap()
    io["outT"] = nc.dram_tensor("outT", [D, TC], F32, kind="ExternalOutput").ap()

    io["sc_xnT"] = [nc.dram_tensor(f"sc_xnT{w}", [128, NKT * 512], BF16).ap() for w in range(NW)]
    io["sc_qtT"] = [nc.dram_tensor(f"sc_qtT{w}", [128, NQT * 512], BF16).ap() for w in range(NW)]
    io["sc_ktT"] = [nc.dram_tensor(f"sc_ktT{w}", [128, NQT * 512], BF16).ap() for w in range(NW)]
    io["sc_OT"] = [nc.dram_tensor(f"sc_OT{w}", [128, NVT * 512], BF16).ap() for w in range(NW)]
    io["sc_xn2T"] = [nc.dram_tensor(f"sc_xn2T{w}", [128, NKT * 512], BF16).ap() for w in range(NW)]
    io["sc_r1T"] = [nc.dram_tensor(f"sc_r1T{w}", [128, NKT * 512], F32).ap() for w in range(NW)]
    io["sc_og"] = [nc.dram_tensor(f"sc_og{w}", [128, NVT * 512], BF16).ap() for w in range(NW)]
    io["sc_w1b"] = [nc.dram_tensor(f"sc_w1b{e_}", [128, NKT * HH], BF16).ap() for e_ in range(E)]
    io["sc_w2b"] = [nc.dram_tensor(f"sc_w2b{e_}", [128, NHT * D], BF16).ap() for e_ in range(E)]
    io["cc_in"] = nc.dram_tensor("cc_in", [128, NQT * 128], F32)
    io["cc_out"] = nc.dram_tensor("cc_out", [256, NQT * 128], F32)

    cmask_np = np.triu(np.ones((128, 128), dtype=np.float32))
    io["c_cmask"] = nc.inline_tensor(cmask_np, name="c_cmask")
    io["c_ident"] = nc.inline_tensor(np.eye(128, dtype=BF), name="c_ident")
    io["c_identf"] = nc.inline_tensor(np.eye(128, dtype=np.float32), name="c_identf")
    sel16 = np.zeros((16, 16 * 128), dtype=np.float32)
    for h in range(16):
        sel16[h, h * 128:(h + 1) * 128] = 1.0
    io["c_sel16"] = nc.inline_tensor(sel16, name="c_sel16")
    sel8 = np.zeros((8, 8 * 128), dtype=ml_dtypes.bfloat16)
    for e_ in range(8):
        sel8[e_, e_ * 128:(e_ + 1) * 128] = 1.0
    io["c_sel8"] = nc.inline_tensor(sel8, name="c_sel8")
    io["c_ones_col"] = nc.inline_tensor(np.ones((128, 1), dtype=BF), name="c_ones_col")
    io["c_ones_colf"] = nc.inline_tensor(np.ones((128, 1), dtype=np.float32), name="c_ones_colf")
    io["c_ones1f"] = nc.inline_tensor(np.ones((1, 128), dtype=np.float32), name="c_ones1f")
    io["c_onesrow"] = nc.inline_tensor(np.ones((1, 512), dtype=BF), name="c_onesrow")

    with ExitStack() as ctx:
        e = ctx.enter_context
        b = Bld(nc, ctx)
        per = {}
        per["zeros"] = e(nc.sbuf_tensor("g_zeros", [128, 512], BF16))
        per["ones_col"] = e(nc.sbuf_tensor("g_ones_col", [128, 1], BF16))
        per["ones_colf"] = e(nc.sbuf_tensor("g_ones_colf", [128, 1], F32))
        per["cbias"] = e(nc.sbuf_tensor("g_cbias", [128, 4], F32))
        per["ones1f"] = e(nc.sbuf_tensor("g_ones1f", [1, 128], F32))
        per["onesrow_bf"] = e(nc.sbuf_tensor("g_onesrow", [1, 512], BF16))
        per["cmask"] = e(nc.sbuf_tensor("g_cmask", [128, 128], F32))
        per["ident"] = e(nc.sbuf_tensor("g_ident", [128, 128], BF16))
        per["identf"] = e(nc.sbuf_tensor("g_identf", [128, 128], F32))
        per["sel8"] = e(nc.sbuf_tensor("g_sel8", [8, 8 * 128], BF16))
        per["expaC"] = e(nc.sbuf_tensor("g_expaC", [128, NG * 8], F32))
        per["lam"] = e(nc.sbuf_tensor("g_lam", [128, NG * 8], F32))
        per["lamrun"] = e(nc.sbuf_tensor("g_lamrun", [128, 8], F32))
        per["normw"] = e(nc.sbuf_tensor("g_normw", [DV, 1], F32))

        tz = b.emit("vector", lambda v: v.memset(per["zeros"][:, :], 0.0), tok=True)
        b.emit("vector", lambda v: v.memset(per["cbias"][:, 0:1], EPS))
        b.emit("vector", lambda v: v.memset(per["cbias"][:, 1:2], 1.0))
        b.emit("vector", lambda v: v.memset(per["cbias"][:, 2:3], LN8))
        tz2 = b.emit("vector", lambda v: v.memset(per["cbias"][:, 3:4], 0.0), tok=True)
        b.emit("sync", lambda s: s.dma_start(out=per["ones_col"][:, :], in_=io["c_ones_col"].ap()), dma="h_const")
        b.emit("sync", lambda s: s.dma_start(out=per["ones_colf"][:, :], in_=io["c_ones_colf"].ap()), dma="h_const")
        b.emit("sync", lambda s: s.dma_start(out=per["ones1f"][:, :], in_=io["c_ones1f"].ap()), dma="h_const")
        b.emit("sync", lambda s: s.dma_start(out=per["onesrow_bf"][:, :], in_=io["c_onesrow"].ap()), dma="h_const")
        b.emit("sync", lambda s: s.dma_start(out=per["cmask"][:, :], in_=io["c_cmask"].ap()), dma="h_const")
        b.emit("sync", lambda s: s.dma_start(out=per["identf"][:, :], in_=io["c_identf"].ap()), dma="h_const")
        b.emit("sync", lambda s: s.dma_start(out=per["sel8"][:, :], in_=io["c_sel8"].ap()), dma="h_const")
        b.emit("sync", lambda s: s.dma_start(out=per["ident"][:, :], in_=io["c_ident"].ap()), dma="h_const")
        tld = b.emit("sync", lambda s: s.dma_start(out=per["normw"][:, :], in_=io["normw"]), dma="h_const")
        b.waitonly("tensor", [tld, tz])
        b.waitonly("vector", [tld])
        b.waitonly("scalar", [tld, tz2])
        b.waitonly("gpsimd", [tld])

        if "p1" in phases:
            phase1(b, io, per, windows=p1_windows, stages=p1_stages)
        if "p2" in phases:
            phase2(b, io, per)
        if "p3a" in phases:
            per["combT"] = e(nc.sbuf_tensor("g_combT", [E, TC], BF16))
        if "cc" in phases or "p3a" in phases:
            with ExitStack() as s3:
                per["Sin"] = s3.enter_context(nc.sbuf_tensor("g_Sin", [128, NQT * 128], F32))
                per["Sinb"] = s3.enter_context(nc.sbuf_tensor("g_Sinb", [128, NQT * 128], BF16))
                per["smask"] = s3.enter_context(nc.sbuf_tensor("g_smask", [128, 1], F32))
                per["sel16"] = s3.enter_context(nc.sbuf_tensor("g_sel16", [16, 16 * 128], F32))
                b.emit("sync", lambda s: s.dma_start(
                    out=per["sel16"][:, :], in_=io["c_sel16"].ap()), dma="h_const2")
                if "cc" in phases:
                    phase_cc(b, io, per)
                if "p3a" in phases:
                    phase3a(b, io, per)
        if "p3b" in phases:
            phase3b(b, io, per)
        if "ffn" in phases:
            phase_ffn(b, io, per)

        for name in dbg:
            if name == "cc_in":
                dout = nc.dram_tensor("dbg_cc_in", [128, NQT * 128], F32,
                                      kind="ExternalOutput").ap()
                b.emit("sync", lambda s, dout=dout: s.dma_start(
                    out=dout, in_=io["cc_in"].ap()), dma="h_dbg")
            elif name in ("sc_xnT", "sc_qtT", "sc_ktT", "sc_OT", "sc_og", "sc_xn2T", "sc_r1T"):
                shp = io[name][0].shape
                dt = F32 if name == "sc_r1T" else BF16
                dout = nc.dram_tensor("dbg_" + name, [NW, shp[0], shp[1]], dt,
                                      kind="ExternalOutput").ap()
                for w in range(NW):
                    b.emit("sync", lambda s, w=w, dout=dout, name=name: s.dma_start(
                        out=dout[w], in_=io[name][w]), dma="h_dbg")
            elif name == "combT":
                dout = nc.dram_tensor("dbg_combT", [E, TC], F32, kind="ExternalOutput").ap()
                dtmp = nc.dram_tensor("dbg_combT_i", [E, TC], BF16)
                b.emit("sync", lambda s, dtmp=dtmp: s.dma_start(
                    out=dtmp.ap(), in_=per["combT"][:, :]), dma="h_dbg")
                b.emit("gpsimd", lambda g, dout=dout, dtmp=dtmp: g.dma_start(
                    out=dout, in_=dtmp.ap()), after=[("h_dbg", b.cnt.get("h_dbg", 0))], dma="g_dbg")
            elif name in ("expaC", "lam"):
                dout = nc.dram_tensor("dbg_" + name, [128, NG * 8], F32,
                                      kind="ExternalOutput").ap()
                b.emit("sync", lambda s, dout=dout, name=name: s.dma_start(
                    out=dout, in_=per[name][:, :]), dma="h_dbg")

        if "ffn" not in phases:
            b.emit("gpsimd", lambda g: g.dma_start(
                out=io["outT"][0:128, 0:512], in_=per["zeros"][:, :]), dma="g_fin")
        b.waitonly("sync", b.alltoks())

        blk = e(nc.Block())
        b.replay(blk)
    return nc


def host_prep(inputs):
    x = inputs["L0_attn_x"]
    f32 = np.float32
    com = dict(
        wq=np.ascontiguousarray(inputs["L0_attn_Wq"], f32),
        wk=np.ascontiguousarray(inputs["L0_attn_Wk"], f32),
        wv=np.ascontiguousarray(inputs["L0_attn_Wv"], f32),
        wg=np.ascontiguousarray(inputs["L0_attn_Wg"], f32),
        wo=np.ascontiguousarray(inputs["L0_attn_Wo"], f32),
        wgk1=np.ascontiguousarray(inputs["L0_attn_Wgk1"], f32),
        wgk2=np.ascontiguousarray(inputs["L0_attn_Wgk2"], f32),
        bgk2=np.ascontiguousarray(inputs["L0_attn_bgk2"].reshape(1, KD), f32),
        normw=np.ascontiguousarray(inputs["L0_attn_norm_w"].reshape(DV, 1), f32),
        gwT=np.ascontiguousarray(inputs["L0_ffn_gate_w"].T, f32),
        gb=np.ascontiguousarray(inputs["L0_ffn_gate_b"].reshape(1, E), f32),
        gbias=np.ascontiguousarray(inputs["L0_ffn_bias"].reshape(1, E), f32),
        w1T=np.ascontiguousarray(inputs["L0_ffn_w1"].transpose(0, 2, 1), f32),
        b1r=np.ascontiguousarray(
            inputs["L0_ffn_b1"].reshape(E, NHT, 128).transpose(2, 0, 1).reshape(128, E * NHT), f32),
        w2T=np.ascontiguousarray(inputs["L0_ffn_w2"].transpose(0, 2, 1), f32),
        b2=np.ascontiguousarray(inputs["L0_ffn_b2"], f32),
    )
    in_maps = []
    for core in range(8):
        bidx, half = core // 2, core % 2
        m = dict(com)
        m["xT"] = np.ascontiguousarray(x[bidx, half * TC:(half + 1) * TC, :].T, f32)
        m["smask"] = np.full((128, 1), float(half), f32)
        in_maps.append(m)
    return in_maps


_NC_CACHE = {}


def _get_nc(phases, dbg=(), **kw):
    key = (tuple(phases), tuple(dbg), tuple(sorted(kw.items())))
    if key not in _NC_CACHE:
        _NC_CACHE[key] = build_nc(phases, dbg, **kw)
    return _NC_CACHE[key]


def run_dev(inputs, phases=("p1",), dbg=(), **kw):
    nc = _get_nc(phases, dbg, **kw)
    in_maps = host_prep(inputs)
    res = run_bass_kernel_spmd(nc, in_maps, core_ids=list(range(8)))
    return res


def kernel(**inputs):
    res = run_dev(inputs, phases=("p1", "p2", "cc", "p3a", "p3b", "ffn"))
    out = np.zeros((4, 8192, D), np.float32)
    for core in range(8):
        bidx, half = core // 2, core % 2
        out[bidx, half * TC:(half + 1) * TC, :] = res.results[core]["outT"].T
    return out
